# version 04
# speedup vs baseline: 1.0578x; 1.0437x over previous
import numpy as np
import contextlib

import concourse.bass as bass
import concourse.mybir as mybir

F32 = mybir.dt.float32
BF16 = mybir.dt.bfloat16
I32 = mybir.dt.int32
U8 = mybir.dt.uint8
AF = mybir.ActivationFunctionType
ALU = mybir.AluOpType
AX = mybir.AxisListType

QUEUES = ("sync", "scalar", "vector", "tensor", "gpsimd")


class Buf:
    __slots__ = ("name", "handle", "space", "shape", "dtype", "writes", "readers", "bid")
    _next_id = [0]

    def __init__(self, name, handle, space, shape, dtype):
        self.name = name
        self.handle = handle
        self.space = space
        self.shape = shape
        self.dtype = dtype
        self.writes = []
        self.readers = []
        self.bid = Buf._next_id[0]
        Buf._next_id[0] += 1

    def ap(self):
        return self.handle[:]

    def __getitem__(self, sl):
        return self.handle[sl]


def _cover(r_new, r_old):
    return (r_new[0] <= r_old[0] and r_new[1] >= r_old[1]
            and r_new[2] <= r_old[2] and r_new[3] >= r_old[3])


def _overlap(a, b):
    return not (a[1] <= b[0] or b[1] <= a[0] or a[3] <= b[2] or b[3] <= a[2])


class R:
    __slots__ = ("buf", "reg")

    def __init__(self, buf, p0=0, p1=None, f0=0, f1=None):
        self.buf = buf
        sh = buf.shape
        fdim = 1
        for s in sh[1:]:
            fdim *= s
        self.reg = (p0, sh[0] if p1 is None else p1, f0, fdim if f1 is None else f1)


class Instr:
    __slots__ = ("iid", "queue", "fn", "is_dma", "deps", "marked", "tag",
                 "waits", "inc_value", "dma_index", "forced", "no_inc")

    def __init__(self, iid, queue, fn, is_dma, tag):
        self.iid = iid
        self.queue = queue
        self.fn = fn
        self.is_dma = is_dma
        self.deps = []
        self.marked = False
        self.tag = tag
        self.waits = []
        self.inc_value = None
        self.dma_index = None
        self.forced = ()
        self.no_inc = False


class Emitter:
    def __init__(self, nc):
        self.nc = nc
        self.instrs = []
        self.q_instrs = {q: [] for q in QUEUES}
        self._slot_of = {}
        self.bufs = []
        self._stack = contextlib.ExitStack()
        self._scopes = []
        self._sbuf_bytes = 0
        self._psum_banks = 0
        self._finalized = False
        self._barrier_deps = []
        self._post_barrier_queues = set(QUEUES)
        self._pending_fence = {}

    def push_scope(self):
        self._scopes.append(contextlib.ExitStack())

    def pop_scope(self):
        self.barrier()
        self._scopes.pop().close()

    def sbuf(self, name, shape, dtype):
        stack = self._scopes[-1] if self._scopes else self._stack
        h = stack.enter_context(self.nc.sbuf_tensor(name, list(shape), dtype))
        b = Buf(name, h, "sbuf", tuple(shape), dtype)
        per_part = int(np.prod(shape[1:])) * mybir.dt.size(dtype)
        self._sbuf_bytes += per_part
        self.bufs.append(b)
        return b

    def psum(self, name, shape, dtype=F32):
        h = self._stack.enter_context(self.nc.psum_tensor(name, list(shape), dtype))
        b = Buf(name, h, "psum", tuple(shape), dtype)
        self._psum_banks += 1
        self.bufs.append(b)
        return b

    def dram_in(self, name, shape, dtype):
        h = self.nc.dram_tensor(name, list(shape), dtype, kind="ExternalInput")
        b = Buf(name, h, "dram", tuple(shape), dtype)
        self.bufs.append(b)
        return b

    def dram_out(self, name, shape, dtype):
        h = self.nc.dram_tensor(name, list(shape), dtype, kind="ExternalOutput")
        b = Buf(name, h, "dram", tuple(shape), dtype)
        self.bufs.append(b)
        return b

    def dram_tmp(self, name, shape, dtype):
        h = self.nc.dram_tensor(name, list(shape), dtype, kind="Internal")
        b = Buf(name, h, "dram", tuple(shape), dtype)
        self.bufs.append(b)
        return b

    def op(self, queue, fn, reads=(), writes=(), is_dma=False, tag="", no_inc=False):
        iid = len(self.instrs)
        ins = Instr(iid, queue, fn, is_dma, tag)
        ins.no_inc = no_inc
        if is_dma:
            nd = sum(1 for i2 in self.q_instrs[queue] if i2.is_dma)
            self._slot_of[iid] = nd % self.DMA_SLOTS

        def _okey(rid):
            ri = ins if rid == iid else self.instrs[rid]
            return (ri.queue, self._slot_of.get(rid)) if ri.is_dma else (ri.queue,)
        for r in list(reads) + list(writes):
            if r.buf.space == "psum":
                r.reg = (0, 128, 0, 1 << 30)
        deps = set()
        if queue not in self._post_barrier_queues:
            deps.update(self._barrier_deps)
            self._post_barrier_queues.add(queue)
        pf = self._pending_fence.pop(queue, None)
        if pf is not None:
            deps.add(pf)
            ins.forced = (pf,)
        for r in reads:
            b = r.buf
            for (wid, wreg) in b.writes:
                if _overlap(r.reg, wreg):
                    deps.add(wid)
            b.readers.append((iid, r.reg))
            if len(b.readers) > 96:
                best = {}
                for (rid, _) in b.readers:
                    k2 = _okey(rid)
                    if rid > best.get(k2, -1):
                        best[k2] = rid
                b.readers = [(rid, (0, 1 << 30, 0, 1 << 30)) for rid in best.values()]
        for w in writes:
            b = w.buf
            for (wid, wreg) in b.writes:
                if _overlap(w.reg, wreg):
                    deps.add(wid)
            for (rid, rreg) in b.readers:
                if _overlap(w.reg, rreg):
                    deps.add(rid)
        for w in writes:
            b = w.buf
            b.writes = [(wid, wreg) for (wid, wreg) in b.writes if not _cover(w.reg, wreg)]
            b.writes.append((iid, w.reg))
            b.readers = [(rid, rreg) for (rid, rreg) in b.readers if not _cover(w.reg, rreg)]
            if len(b.writes) > 96:
                best = {}
                for (wid, _) in b.writes:
                    k2 = _okey(wid)
                    if wid > best.get(k2, -1):
                        best[k2] = wid
                b.writes = [(wid, (0, 1 << 30, 0, 1 << 30)) for wid in best.values()]
        deps.discard(iid)
        ins.deps = sorted(deps)
        self.instrs.append(ins)
        self.q_instrs[queue].append(ins)
        return ins

    def fence(self, queue):
        qi = self.q_instrs[queue]
        if qi:
            self._pending_fence[queue] = qi[-1].iid

    def barrier(self):
        deps = []
        for q in QUEUES:
            qi = self.q_instrs[q]
            if not qi:
                continue
            deps.append(qi[-1].iid)
            seen = set()
            got_compute = qi[-1].is_dma is False
            for ins in reversed(qi):
                if ins.is_dma:
                    sl = self._slot_of[ins.iid]
                    if sl not in seen:
                        seen.add(sl)
                        deps.append(ins.iid)
                elif not got_compute:
                    got_compute = True
                    deps.append(ins.iid)
                if len(seen) == self.DMA_SLOTS and got_compute:
                    break
        self._barrier_deps = deps
        self._post_barrier_queues = set()

    def sbuf_at(self, name, shape, dtype, offset):
        h = self.nc.alloc_sbuf_tensor_at(name, list(shape), dtype, offset=offset)
        b = Buf(name, h, "sbuf", tuple(shape), dtype)
        self.bufs.append(b)
        return b

    def dma(self, queue, out_ap, in_ap, reads, writes, tag="", **kw):
        def fn(e, _o=out_ap, _i=in_ap, _kw=kw):
            return e.dma_start(_o, _i, **_kw)
        return self.op(queue, fn, reads=reads, writes=writes, is_dma=True, tag=tag)

    DMA_SLOTS = 4

    def emit(self, final_queue="sync", strict_sync=False, same_engine_sync=True):
        assert not self._finalized
        self._finalized = True
        nc = self.nc
        S = self.DMA_SLOTS
        for q in QUEUES:
            di = 0
            for ins in self.q_instrs[q]:
                if ins.is_dma:
                    ins.dma_index = di
                    di += 1
        by_iid = self.instrs
        qnext = {}
        for q in QUEUES:
            qi = self.q_instrs[q]
            nxt = [None] * len(qi)
            later = None
            for i in range(len(qi) - 1, -1, -1):
                nxt[i] = later
                if not qi[i].no_inc:
                    later = qi[i]
            qpos = {ins.iid: i for i, ins in enumerate(qi)}
            qnext[q] = (nxt, qpos)
        for ins in self.instrs:
            for d in ins.deps:
                dep = by_iid[d]
                if dep.no_inc:
                    nxt, qpos = qnext[dep.queue]
                    later = nxt[qpos[dep.iid]]
                    later = later
                    j = qpos[dep.iid]
                    cand = None
                    qi = self.q_instrs[dep.queue]
                    for k2 in range(j + 1, len(qi)):
                        if not qi[k2].no_inc:
                            cand = qi[k2]
                            break
                    assert cand is not None, "no inc-able instr after no_inc dep"
                    dep = cand
                    d = dep.iid
                if (dep.queue == ins.queue and not dep.is_dma and not strict_sync
                        and d not in ins.forced):
                    if not (same_engine_sync and ins.queue != "tensor"):
                        continue
                if (dep.queue == ins.queue and dep.is_dma and ins.is_dma
                        and dep.dma_index % S == ins.dma_index % S):
                    continue
                if dep.is_dma or not (dep.queue == ins.queue and strict_sync and not dep.is_dma):
                    dep.marked = True
                else:
                    dep.marked = True
        cval = {}
        for q in QUEUES:
            c = 0
            for ins in self.q_instrs[q]:
                if ins.marked and not ins.is_dma:
                    c += 1
                    ins.inc_value = c
                cval[ins.iid] = c
        dkey = {}
        for q in QUEUES:
            for ins in self.q_instrs[q]:
                if ins.is_dma:
                    slot = ins.dma_index % S
                    use = ins.dma_index // S
                    dkey[ins.iid] = (("d", q, slot), 16 * (use + 1))
        marks = {q: {} for q in QUEUES}
        for ins in self.instrs:
            need = {}
            if ins.is_dma and ins.dma_index >= S:
                key = ("d", ins.queue, ins.dma_index % S)
                v = 16 * (ins.dma_index // S)
                need[key] = max(need.get(key, 0), v)
            for d in ins.deps:
                dep = by_iid[d]
                if (dep.queue == ins.queue and not dep.is_dma and not strict_sync
                        and d not in ins.forced):
                    if not (same_engine_sync and ins.queue != "tensor"):
                        continue
                if (dep.queue == ins.queue and dep.is_dma and ins.is_dma
                        and dep.dma_index % S == ins.dma_index % S):
                    continue
                if dep.is_dma:
                    key, v = dkey[dep.iid]
                else:
                    key = ("c", dep.queue)
                    v = cval[dep.iid]
                    if v == 0:
                        continue
                if need.get(key, 0) < v:
                    need[key] = v
            wm = marks[ins.queue]
            for key, v in sorted(need.items(), key=str):
                if wm.get(key, 0) < v:
                    ins.waits.append((key, v))
                    wm[key] = v

        dma_totals = {}
        for q in QUEUES:
            nd = sum(1 for i in self.q_instrs[q] if i.is_dma)
            for slot in range(S):
                uses = (nd - slot + S - 1) // S
                if uses > 0:
                    dma_totals[("d", q, slot)] = 16 * uses
        final_waits = []
        for q in QUEUES:
            if q == final_queue:
                continue
            qi = self.q_instrs[q]
            if qi:
                last = qi[-1]
                if not last.is_dma and last.inc_value is None:
                    last.marked = True
                    prev = 0
                    for ins in qi:
                        if ins.inc_value is not None:
                            prev = max(prev, ins.inc_value)
                    last.inc_value = prev + 1
                if not last.is_dma:
                    final_waits.append((("c", q), last.inc_value))
        for key, v in dma_totals.items():
            final_waits.append((key, v))

        sems = {}
        for q in QUEUES:
            if any((i.inc_value is not None) for i in self.q_instrs[q]):
                sems[("c", q)] = self._stack.enter_context(nc.semaphore("c_" + q))
        for key in dma_totals:
            sems[key] = self._stack.enter_context(
                nc.semaphore(f"d_{key[1]}_{key[2]}"))

        block = self._stack.enter_context(nc.Block())

        def body_for(q):
            def body(e):
                for ins in self.q_instrs[q]:
                    for (key, v) in ins.waits:
                        e.wait_ge(sems[key], v)
                    bi = ins.fn(e)
                    if ins.is_dma:
                        bi.then_inc(sems[("d", q, ins.dma_index % self.DMA_SLOTS)], 16)
                    elif ins.inc_value is not None:
                        bi.then_inc(sems[("c", q)], 1)
                if q == final_queue:
                    seen = {}
                    for key, v in final_waits:
                        if key in sems and seen.get(key, 0) < v:
                            e.wait_ge(sems[key], v)
                            seen[key] = v
            return body

        for q in QUEUES:
            if self.q_instrs[q] or q == final_queue:
                getattr(block, q)(body_for(q))

    def close(self):
        self._stack.close()

    def stats(self):
        nmm = sum(1 for i in self.instrs if i.queue == "tensor")
        nd = sum(1 for i in self.instrs if i.is_dma)
        nw = sum(len(i.waits) for i in self.instrs)
        return (f"instrs={len(self.instrs)} tensor={nmm} dma={nd} waits={nw} "
                f"sbuf_kb_per_part={self._sbuf_bytes/1024:.1f} psum_banks={self._psum_banks}")



def panelize(w, p_out=512):
    d_in, d_out = w.shape
    assert d_in % 128 == 0 and d_out % p_out == 0, (d_in, d_out, p_out)
    kt = d_in // 128
    np_ = d_out // p_out
    v = w.reshape(kt, 128, np_, p_out).transpose(2, 1, 0, 3).reshape(np_, 128, kt * p_out)
    return np.ascontiguousarray(v)


def vec_mt(v):
    return np.ascontiguousarray(np.asarray(v, np.float32).reshape(-1, 128).T)


def ceil_div(a, b):
    return (a + b - 1) // b



class V:
    __slots__ = ("buf", "ap", "reg")

    def __init__(self, buf, ap, reg):
        self.buf = buf
        self.ap = ap
        self.reg = reg

    def r(self):
        x = R(self.buf)
        x.reg = self.reg
        return x


def vw(buf, mt=None, l0=0, nl=None, p0=0, p1=None):
    sh = buf.shape
    if p1 is None:
        p1 = sh[0]
    if len(sh) == 3:
        if mt is None:
            L = sh[1] * sh[2]
            if nl is None:
                nl = L - l0
            ap = buf.handle[p0:p1] if (l0 == 0 and nl == L) else None
            assert ap is not None, "3D whole-buffer view must cover all free"
            reg = (p0, p1, 0, L)
            return V(buf, ap, reg)
        L = sh[2]
        if nl is None:
            nl = L - l0
        ap = buf.handle[p0:p1, mt, l0:l0 + nl]
        reg = (p0, p1, mt * L + l0, mt * L + l0 + nl)
    else:
        L = int(np.prod(sh[1:]))
        if nl is None:
            nl = L - l0
        ap = buf.handle[p0:p1, l0:l0 + nl]
        reg = (p0, p1, l0, l0 + nl)
    return V(buf, ap, reg)


def _mkR(buf, reg):
    x = R(buf)
    x.reg = reg
    return x


class Ops:

    def __init__(self, em):
        self.em = em

    def _o(self, q, fn, ins, outs, tag=""):
        def norm(v):
            if isinstance(v, V):
                return v.r()
            if isinstance(v, R):
                return v
            return R(v)
        return self.em.op(q, fn, reads=[norm(v) for v in ins],
                          writes=[norm(v) for v in outs], tag=tag)

    def tt(self, q, out, a, b, op, tag=""):
        return self._o(q, (lambda _o, _a, _b, _op: lambda e: e.tensor_tensor(_o, _a, _b, _op))(
            out.ap, a.ap, b.ap, op), [a, b], [out], tag)

    def ts(self, q, out, a, s1, op0, s2=None, op1=ALU.bypass, tag=""):
        ins = [a]
        s1v = s1.ap if isinstance(s1, V) else s1
        s2v = s2.ap if isinstance(s2, V) else s2
        if isinstance(s1, V):
            ins.append(s1)
        if isinstance(s2, V):
            ins.append(s2)
        return self._o(q, (lambda _o, _a, _s1, _s2, _op0, _op1:
                           lambda e: e.tensor_scalar(_o, _a, _s1, _s2, _op0, _op1))(
            out.ap, a.ap, s1v, s2v, op0, op1), ins, [out], tag)

    def stt(self, q, out, a, s, b, op0, op1, tag=""):
        ins = [a, b]
        sv = s.ap if isinstance(s, V) else s
        if isinstance(s, V):
            ins.append(s)
        return self._o(q, (lambda _o, _a, _s, _b, _op0, _op1:
                           lambda e: e.scalar_tensor_tensor(_o, _a, _s, _b, _op0, _op1))(
            out.ap, a.ap, sv, b.ap, op0, op1), ins, [out], tag)

    def act(self, out, in_, func, bias=0.0, scale=1.0, accum=None, q="scalar", tag=""):
        ins = [in_]
        bv = bias.ap if isinstance(bias, V) else bias
        sv = scale.ap if isinstance(scale, V) else scale
        if isinstance(bias, V):
            ins.append(bias)
        if isinstance(scale, V):
            ins.append(scale)
        outs = [out]
        av = None
        if accum is not None:
            av = accum.ap
            outs.append(accum)
        return self._o(q, (lambda _o, _i, _f, _b, _s, _acc:
                           lambda e: e.activation(_o, _i, _f, bias=_b, scale=_s,
                                                  accum_out=_acc))(
            out.ap, in_.ap, func, bv, sv, av), ins, outs, tag)

    def copy(self, q, out, in_, tag=""):
        if q == "scalar":
            return self._o(q, (lambda _o, _i: lambda e: e.copy(_o, _i))(
                out.ap, in_.ap), [in_], [out], tag)
        return self._o(q, (lambda _o, _i: lambda e: e.tensor_copy(_o, _i))(
            out.ap, in_.ap), [in_], [out], tag)

    def memset(self, q, out, val, tag=""):
        return self._o(q, (lambda _o, _v: lambda e: e.memset(_o, _v))(
            out.ap, float(val)), [], [out], tag)

    def reduce(self, q, out, in_, op, negate=False, tag=""):
        return self._o(q, (lambda _o, _i, _op, _n: lambda e: e.tensor_reduce(
            _o, _i, AX.X, _op, negate=_n))(out.ap, in_.ap, op, negate), [in_], [out], tag)

    def scan_mult_add(self, q, out, a, b, initial=0.0, tag=""):
        iv = initial.ap if isinstance(initial, V) else initial
        ins = [a, b] + ([initial] if isinstance(initial, V) else [])
        return self._o(q, (lambda _o, _a, _b, _iv: lambda e: e.tensor_tensor_scan(
            _o, _a, _b, _iv, ALU.mult, ALU.add))(out.ap, a.ap, b.ap, iv), ins, [out], tag)

    def iota(self, out, pattern, base=0, channel_multiplier=0, allow_imprecise=False,
             tag=""):
        return self._o("gpsimd", (lambda _o, _p, _b, _c, _a: lambda e: e.iota(
            _o, _p, base=_b, channel_multiplier=_c,
            allow_small_or_imprecise_dtypes=_a))(out.ap, pattern, base,
                                                 channel_multiplier, allow_imprecise),
            [], [out], tag)

    def affine_select(self, out, in_, pattern, cmp, fill, base=0, channel_multiplier=0, tag=""):
        return self._o("gpsimd", (lambda _o, _i, _p, _c, _f, _b, _cm:
                                  lambda e: e.affine_select(_o, _i, _p, _c, _f, base=_b,
                                                            channel_multiplier=_cm))(
            out.ap, in_.ap, pattern, cmp, fill, base, channel_multiplier), [in_], [out], tag)

    def mm(self, out, lhsT, rhs, start=True, stop=True, tile_position=None, tag=""):
        fn = (lambda _o, _l, _r, _st, _sp, _tp:
              lambda e: e.matmul(_o, _l, _r, start=_st, stop=_sp,
                                 tile_position=_tp))(
            out.ap, lhsT.ap, rhs.ap, start, stop, tile_position)
        ins = [lhsT, rhs]
        return self.em.op("tensor", fn, reads=[i.r() for i in ins], writes=[out.r()],
                          tag=tag, no_inc=not stop)

    def transpose(self, out_psum, in_, ident, tag=""):
        id_ap = ident.ap if isinstance(ident, V) else ident.handle[:]
        return self._o("tensor", (lambda _o, _i, _id: lambda e: e.transpose(_o, _i, _id))(
            out_psum.ap, in_.ap, id_ap), [in_, ident], [out_psum], tag)

    def dma(self, q, out, in_, tag=""):
        return self.em.op(q, (lambda _o, _i: lambda e: e.dma_start(_o, _i))(out.ap, in_.ap),
                          reads=[in_.r()], writes=[out.r()], is_dma=True, tag=tag)



class Linear:

    def __init__(self, em, name, w_np, p_out=512):
        self.d_in, self.d_out = w_np.shape
        self.p_out = p_out
        self.kt = self.d_in // 128
        self.n_pan = self.d_out // p_out
        self.pan_np = panelize(np.asarray(w_np, np.float32), p_out)
        self.w = em.dram_in(name + "_w", self.pan_np.shape, F32)

    def in_map(self):
        return {self.w.name: self.pan_np}


def emit_linear(em, lin, src, dst, n_tok, w_bufs, psum_pool, *,
                act=None, scale=None, bias=None, evac_fn=None, out_q="scalar",
                lg_size=512, tok0=0, src_tok0=None, psum_start_idx=0,
                pan_range=None, dst_mt0=0, src_ct0=0, tag=""):
    o = Ops(em)
    kt, p_out, n_pan = lin.kt, lin.p_out, lin.n_pan
    mpp = p_out // 128
    n_lg = ceil_div(n_tok, lg_size)
    if src_tok0 is None:
        src_tok0 = tok0
    lo, hi = pan_range if pan_range is not None else (0, n_pan)
    pi = psum_start_idx
    for pan in range(lo, hi):
        wb = w_bufs[pan % len(w_bufs)]
        em.dma("gpsimd", wb.handle[:, 0:kt * p_out], lin.w[pan],
               reads=[R(lin.w)], writes=[_mkR(wb, (0, 128, 0, kt * p_out))],
               tag=f"{tag}wdma")
        for lg in range(n_lg):
            l0 = lg * lg_size
            nl = min(lg_size, n_tok - l0)
            for m in range(mpp):
                mt_rel = (pan - lo) * mpp + m + dst_mt0
                mt_g = pan * mpp + m
                psb = psum_pool[pi % len(psum_pool)]
                pi += 1
                for k in range(kt):
                    o.mm(vw(psb, l0=0, nl=nl),
                         V(wb, wb.handle[:, k * p_out + m * 128: k * p_out + (m + 1) * 128],
                           (0, 128, k * p_out + m * 128, k * p_out + (m + 1) * 128)),
                         vw(src, mt=src_ct0 + k, l0=src_tok0 + l0, nl=nl),
                         start=(k == 0), stop=(k == kt - 1), tag=f"{tag}mm")
                if evac_fn is not None:
                    evac_fn(em, psb, mt_g, mt_rel, lg, l0, nl)
                else:
                    sc = (V(scale, scale.handle[:, mt_g:mt_g + 1], (0, 128, mt_g, mt_g + 1))
                          if isinstance(scale, Buf) else (1.0 if scale is None else scale))
                    bi = (V(bias, bias.handle[:, mt_g:mt_g + 1], (0, 128, mt_g, mt_g + 1))
                          if isinstance(bias, Buf) else (0.0 if bias is None else bias))
                    fn = act if act is not None else AF.Copy
                    o.act(vw(dst, mt=mt_rel, l0=tok0 + l0, nl=nl),
                          vw(psb, l0=0, nl=nl), fn, bias=bi, scale=sc, q=out_q,
                          tag=f"{tag}ev")
    return pi


def emit_transpose_fm_from_tm(em, src_tm, dst_fm, ident, psum_tr, d, tok0=0,
                              n_rows=128, src_f0=0, tag="tr"):
    o = Ops(em)
    ct_n = d // 128
    for ct in range(ct_n):
        psb = psum_tr[ct % len(psum_tr)]
        o.transpose(vw(psb, l0=0, nl=n_rows),
                    vw(src_tm, l0=src_f0 + ct * 128, nl=128, p0=0, p1=n_rows),
                    ident, tag=tag + "T")
        o.copy("vector", vw(dst_fm, mt=ct, l0=tok0, nl=n_rows), vw(psb, l0=0, nl=n_rows),
               tag=tag + "E")


"""Stage emitters for mk_recurrent_d2 module (per-core program, batch-parallel).

Layout conventions per core (T=4096 tokens, D=1024):
  - token-major (tm) tiles: [128 tokens, D] ; full-seq tm scratch in HBM: [32, 128, D]
  - feature-major (fm): [128 features(part), ct, L] in SBUF; HBM fm scratch: [CT, 128, L]
  - per-out-channel vectors ("mt-vec"): sbuf [128, d//128] (vec_mt on host)
  - routing/token-column-major (tcm): [128, 32], token = col*128 + row ... NOTE:
    we use token = tile*128 + p with tile as the free column -> tcm[p, tile].
"""
import numpy as np


T = 4096
D = 1024
NT = T // 128
CT = D // 128


class MK:

    def __init__(self, nc, wb_cols=8192):
        self.nc = nc
        self.em = Emitter(nc)
        self.o = Ops(self.em)
        em = self.em
        self.ident = em.sbuf("identb", [128, 128], BF16)
        self.identf = em.sbuf("identf", [128, 128], F32)
        self.ones_col = em.sbuf("ones_col", [128, 2], F32)
        self.onesb = em.sbuf("onesb", [128, 2], BF16)
        self.small = em.sbuf("small", [128, 64], F32)
        self.tf32 = [em.sbuf(f"mk_tf{i}", [128, D], F32) for i in range(2)]
        self.tbf = [em.sbuf(f"mk_tb{i}", [128, D], BF16) for i in range(2)]
        self.dump = em.sbuf("mk_dump", [128, D], F32)
        self.stats = em.sbuf("mk_stats", [128, 16], F32)
        self.wb = ([em.sbuf(f"mk_wb{i}", [128, wb_cols], BF16) for i in range(2)]
                   if wb_cols else None)
        self.ps = [em.psum(f"ps{i}", [128, 512], F32) for i in range(4)]
        self.ptrb = [em.psum(f"ptrb{i}", [128, 128], BF16) for i in range(2)]
        self.ps2 = [em.psum(f"ps2_{i}", [128, 512], F32) for i in range(2)]
        self.in_map = {}
        cvals = [1e-5, 1e-6]
        carr = np.zeros((128, 4), np.float32)
        for i, v in enumerate(cvals):
            carr[:, i] = v
        cd = self.dram_in("constv_hbm", carr)
        self.const_sb = em.sbuf("constv", [128, 4], F32)
        em.dma("sync", self.const_sb.ap(), cd.ap(), reads=[R(cd)],
               writes=[R(self.const_sb)], tag="ldconst")
        for i, v in enumerate(cvals):
            nc.const_aps.register(F32, v, self.const_sb.handle[:, i:i + 1])
        em.barrier()

    def dram_in(self, name, arr):
        dt = {np.dtype(np.float32): F32}.get(arr.dtype, BF16)
        b = self.em.dram_in(name, list(arr.shape), dt)
        self.in_map[name] = arr
        return b

    def load_const(self, name, arr, dtype=None):
        em = self.em
        arr = np.ascontiguousarray(arr)
        d = self.dram_in(name + "_hbm", arr)
        sb = em.sbuf(name, list(arr.shape), F32 if arr.dtype == np.float32 else BF16)
        em.dma("sync", sb.ap(), d.ap(), reads=[R(d)], writes=[R(sb)], tag="ldc")
        return sb



def newton_recip(mk, dst, src, tmp, q="vector"):
    o = mk.o
    o.act(dst, src, AF.Ln, tag="nrL")
    o.act(dst, dst, AF.Exp, scale=-1.0, tag="nrE")
    o.tt(q, tmp, src, dst, ALU.mult, tag="nr1")
    o.ts(q, tmp, tmp, -1.0, ALU.mult, 2.0, ALU.add, tag="nr2")
    o.tt(q, dst, dst, tmp, ALU.mult, tag="nr3")


def newton_rsqrt(mk, dst, src, tmp, eps=0.0, q="vector"):
    o = mk.o
    o.act(dst, src, AF.Ln, bias=float(eps), tag="nsL")
    o.act(dst, dst, AF.Exp, scale=-0.5, tag="nsE")
    if eps != 0.0:
        o.ts(q, tmp, src, float(eps), ALU.add, tag="nsA")
        srcv = tmp
        tmp2 = dst
    else:
        srcv = src
    o.tt(q, tmp, srcv, dst, ALU.mult, tag="ns1")
    o.tt(q, tmp, tmp, dst, ALU.mult, tag="ns2")
    o.ts(q, tmp, tmp, -0.5, ALU.mult, 1.5, ALU.add, tag="ns3")
    o.tt(q, dst, dst, tmp, ALU.mult, tag="ns4")





def stage_ln0(mk, x_in, xn0_tm_scr, xn0_fm_sb, s_q, s_kv, rtq_b, rtkv_b, Rq_col, Rkv_col,
              eps=1e-5, tag="ln0", ln2=None):
    em, o = mk.em, mk.o
    xt = [mk.tf32[0], mk.dump]
    xnb = mk.tbf
    dump = mk.tf32[1]
    st = mk.stats
    xr = x_in.handle[:].rearrange("(t p) d -> t p d", p=128)
    for t in range(NT):
        xb = xt[t % 2]
        em.dma("sync", xb.ap(), xr[t], reads=[R(x_in)], writes=[R(xb)], tag=tag + "ld")
        C = lambda j: V(st, st.handle[:, j:j + 1], (0, 128, j, j + 1))
        o.act(vw(dump), vw(xb), AF.Square, accum=C(0), tag=tag + "sq")
        o.reduce("vector", C(1), vw(xb), ALU.add, tag=tag + "mu")
        o.ts("vector", C(2), C(1), 1.0 / D, ALU.mult, tag=tag + "m")
        o.tt("vector", C(3), C(2), C(2), ALU.mult, tag=tag + "m2")
        o.ts("vector", C(4), C(0), 1.0 / D, ALU.mult, tag=tag + "v1")
        o.tt("vector", C(4), C(4), C(3), ALU.subtract, tag=tag + "var")
        o.act(C(5), C(4), AF.Ln, bias=float(eps), tag=tag + "nl")
        o.act(C(5), C(5), AF.Exp, scale=-0.5, tag=tag + "ne")
        o.ts("vector", C(6), C(4), float(eps), ALU.add, tag=tag + "na")
        o.tt("vector", C(6), C(6), C(5), ALU.mult, tag=tag + "n1")
        o.tt("vector", C(6), C(6), C(5), ALU.mult, tag=tag + "n2")
        o.ts("vector", C(6), C(6), -0.5, ALU.mult, 1.5, ALU.add, tag=tag + "n3")
        o.tt("vector", C(5), C(5), C(6), ALU.mult, tag=tag + "n4")
        xo = xnb[t % 2]
        o.ts("vector", vw(xo), vw(xb), C(2), ALU.subtract, C(5), ALU.mult, tag=tag + "xn")
        for (rb, Rc, s_sb, j) in ((rtq_b, Rq_col, s_q, 7), (rtkv_b, Rkv_col, s_kv, 10)):
            o.stt("vector", vw(dump), vw(xb), 1.0, vw(rb), ALU.mult, ALU.mult,
                  tag=tag + "dot")
            o.reduce("vector", C(j), vw(dump), ALU.add, tag=tag + "dr")
            o.tt("vector", C(j + 1), C(2), Rc, ALU.mult, tag=tag + "mr")
            o.tt("vector", C(j + 1), C(j), C(j + 1), ALU.subtract, tag=tag + "ts")
            o.tt("vector", V(s_sb, s_sb.handle[:, t:t + 1], (0, 128, t, t + 1)),
                 C(j + 1), C(5), ALU.mult, tag=tag + "s")
        em.dma("sync", xn0_tm_scr.handle[t], xo.ap(),
               reads=[R(xo)], writes=[R(xn0_tm_scr, 0, 128, t * D, (t + 1) * D)],
               tag=tag + "st")
        if ln2 is not None:
            o.tt("vector", C(13), C(4), C(5), ALU.mult, tag=tag + "w1")
            o.tt("vector", C(13), C(13), C(5), ALU.mult, tag=tag + "w2")
            newton_rsqrt(mk, C(14), C(13), C(15), eps=float(ln2["eps"]))
            o.tt("vector", C(14), C(14), C(5), ALU.mult, tag=tag + "w3")
            o.ts("vector", vw(dump), vw(xb), C(2), ALU.subtract, C(14), ALU.mult,
                 tag=tag + "w4")
            o.tt("vector", vw(dump), vw(dump), vw(ln2["gamma_b"]), ALU.mult,
                 tag=tag + "w5")
            x2 = ln2["buf"][t % 2]
            o.tt("vector", vw(x2), vw(dump), vw(ln2["beta_b"]), ALU.add,
                 tag=tag + "w6")
            em.dma("sync", ln2["tm_scr"].handle[t], x2.ap(), reads=[R(x2)],
                   writes=[R(ln2["tm_scr"], 0, 128, t * D, (t + 1) * D)],
                   tag=tag + "w7")
        if xn0_fm_sb is not None:
            emit_transpose_fm_from_tm(em, xo, xn0_fm_sb, mk.ident, mk.ptrb, D,
                                      tok0=t * 128, tag=tag + "tr")



def stage_combine(mk, xprev_in, xprev_is_f32_tokmajor, s_tm_scr, g_tm_scr,
                  x1_tm_scr, out=None, ln_outs=None, eps=1e-5, g_is_f32=False,
                  tag="cmb"):
    em, o = mk.em, mk.o
    xt = [em.sbuf(f"{tag}_x{i}", [128, D], F32) for i in range(2)]
    stt_ = [em.sbuf(f"{tag}_s{i}", [128, D], BF16) for i in range(2)]
    gt = [em.sbuf(f"{tag}_g{i}", [128, D], F32 if g_is_f32 else BF16)
          for i in range(2)]
    res = [em.sbuf(f"{tag}_r{i}", [128, D], F32) for i in range(2)]
    gs = em.sbuf(f"{tag}_gs", [128, D], F32)
    st = em.sbuf(f"{tag}_stat", [128, 16], F32)
    xnb = [em.sbuf(f"{tag}_xnb{i}", [128, D], BF16) for i in range(2)]
    if ln_outs is not None:
        ln_outs.setdefault("dump", mk.dump)
        ln_outs.setdefault("xn_buf", xnb)
    if xprev_is_f32_tokmajor:
        xr = xprev_in.handle[:].rearrange("(t p) d -> t p d", p=128)
    else:
        xr = xprev_in.handle
    for t in range(NT):
        xb, sb, gb, rb = xt[t % 2], stt_[t % 2], gt[t % 2], res[t % 2]
        em.dma("sync", xb.ap(), xr[t], reads=[R(xprev_in)], writes=[R(xb)], tag=tag + "lx")
        em.dma("sync", sb.ap(), s_tm_scr.handle[t], reads=[R(s_tm_scr)], writes=[R(sb)],
               tag=tag + "ls")
        em.dma("sync", gb.ap(), g_tm_scr.handle[t], reads=[R(g_tm_scr)], writes=[R(gb)],
               tag=tag + "lg")
        o.act(vw(gs), vw(gb), AF.Silu, q="scalar", tag=tag + "si")
        o.tt("vector", vw(gs), vw(gs), vw(sb), ALU.mult, tag=tag + "sg")
        o.tt("vector", vw(rb), vw(xb), vw(gs), ALU.add, tag=tag + "add")
        if x1_tm_scr is not None:
            em.dma("sync", x1_tm_scr.handle[t], rb.ap(), reads=[R(rb)],
                   writes=[R(x1_tm_scr, 0, 128, t * D, (t + 1) * D)], tag=tag + "sx")
        if out is not None:
            outr = out.handle[:].rearrange("(t p) d -> t p d", p=128)
            em.dma("sync", outr[t], rb.ap(), reads=[R(rb)],
                   writes=[R(out)], tag=tag + "so")
        if ln_outs is not None:
            _emit_ln_tile(mk, rb, st, ln_outs, t, eps, tag)


def _emit_ln_tile(mk, rb, st, lo, t, eps, tag):
    em, o = mk.em, mk.o
    C = lambda j: V(st, st.handle[:, j:j + 1], (0, 128, j, j + 1))
    dump = lo["dump"]
    xnb = lo["xn_buf"][t % 2]
    o.act(vw(dump), vw(rb), AF.Square, accum=C(0), tag=tag + "q")
    o.reduce("vector", C(1), vw(rb), ALU.add, tag=tag + "r")
    o.ts("vector", C(2), C(1), 1.0 / D, ALU.mult, tag=tag + "m")
    o.tt("vector", C(3), C(2), C(2), ALU.mult, tag=tag + "m2")
    o.ts("vector", C(4), C(0), 1.0 / D, ALU.mult, tag=tag + "v")
    o.tt("vector", C(4), C(4), C(3), ALU.subtract, tag=tag + "v2")
    o.act(C(5), C(4), AF.Ln, bias=float(eps), tag=tag + "l")
    o.act(C(5), C(5), AF.Exp, scale=-0.5, tag=tag + "e")
    o.ts("vector", C(6), C(4), float(eps), ALU.add, tag=tag + "a")
    o.tt("vector", C(6), C(6), C(5), ALU.mult, tag=tag + "1")
    o.tt("vector", C(6), C(6), C(5), ALU.mult, tag=tag + "2")
    o.ts("vector", C(6), C(6), -0.5, ALU.mult, 1.5, ALU.add, tag=tag + "3")
    o.tt("vector", C(5), C(5), C(6), ALU.mult, tag=tag + "4")
    if lo.get("xn_f32_buf") is not None:
        xf = lo["xn_f32_buf"]
        o.ts("vector", vw(xf), vw(rb), C(2), ALU.subtract, C(5), ALU.mult, tag=tag + "xf")
        o.copy("scalar", vw(xnb), vw(xf), tag=tag + "cb")
    else:
        xf = None
        o.ts("vector", vw(xnb), vw(rb), C(2), ALU.subtract, C(5), ALU.mult, tag=tag + "xn")
    if lo.get("xn_tm_scr") is not None:
        em.dma("sync", lo["xn_tm_scr"].handle[t], xnb.ap(), reads=[R(xnb)],
               writes=[R(lo["xn_tm_scr"], 0, 128, t * D, (t + 1) * D)], tag=tag + "sx")
    if lo.get("xn_fm_sb") is not None:
        emit_transpose_fm_from_tm(em, xnb, lo["xn_fm_sb"], mk.ident, mk.ptrb, D,
                                  tok0=t * 128, tag=tag + "tr")
    if lo.get("ln1") is not None:
        l1 = lo["ln1"]
        src = xf if xf is not None else xnb
        o.act(vw(dump), vw(src), AF.Square, accum=C(8), tag=tag + "q2")
        o.reduce("vector", C(9), vw(src), ALU.add, tag=tag + "r2")
        o.ts("vector", C(10), C(9), 1.0 / D, ALU.mult, tag=tag + "m_2")
        o.tt("vector", C(11), C(10), C(10), ALU.mult, tag=tag + "m22")
        o.ts("vector", C(12), C(8), 1.0 / D, ALU.mult, tag=tag + "v_2")
        o.tt("vector", C(12), C(12), C(11), ALU.subtract, tag=tag + "v22")
        o.act(C(13), C(12), AF.Ln, bias=float(l1["eps"]), tag=tag + "l2")
        o.act(C(13), C(13), AF.Exp, scale=-0.5, tag=tag + "e2")
        o.ts("vector", C(14), C(12), float(l1["eps"]), ALU.add, tag=tag + "a2")
        o.tt("vector", C(14), C(14), C(13), ALU.mult, tag=tag + "12")
        o.tt("vector", C(14), C(14), C(13), ALU.mult, tag=tag + "22")
        o.ts("vector", C(14), C(14), -0.5, ALU.mult, 1.5, ALU.add, tag=tag + "32")
        o.tt("vector", C(13), C(13), C(14), ALU.mult, tag=tag + "42")
        lb = l1["buf"][t % 2]
        o.ts("vector", vw(dump), vw(src), C(10), ALU.subtract, C(13), ALU.mult,
             tag=tag + "x2")
        o.tt("vector", vw(dump), vw(dump), vw(l1["gamma_b"]), ALU.mult, tag=tag + "ga")
        o.tt("vector", vw(lb), vw(dump), vw(l1["beta_b"]), ALU.add, tag=tag + "be")
        if l1.get("fm_sb") is not None:
            emit_transpose_fm_from_tm(em, lb, l1["fm_sb"], mk.ident, mk.ptrb, D,
                                      tok0=t * 128, tag=tag + "t2")
        if l1.get("tm_scr") is not None:
            em.dma("sync", l1["tm_scr"].handle[t], lb.ap(), reads=[R(lb)],
                   writes=[R(l1["tm_scr"], 0, 128, t * D, (t + 1) * D)], tag=tag + "s2")



def stage_ln_affine(mk, src_tm_scr, dst_fm_sb, gamma_b, beta_b, eps, tag="lna",
                    dst_tm_scr=None):
    em, o = mk.em, mk.o
    xt = mk.tf32
    ob = mk.tbf
    dump = mk.dump
    st = mk.stats
    for t in range(NT):
        xb, oo = xt[t % 2], ob[t % 2]
        em.dma("gpsimd", xb.ap(), src_tm_scr.handle[t], reads=[R(src_tm_scr)],
               writes=[R(xb)], tag=tag + "ld")
        C = lambda j: V(st, st.handle[:, j:j + 1], (0, 128, j, j + 1))
        o.act(vw(dump), vw(xb), AF.Square, accum=C(0), tag=tag + "q")
        o.reduce("vector", C(1), vw(xb), ALU.add, tag=tag + "r")
        o.ts("vector", C(2), C(1), 1.0 / D, ALU.mult, tag=tag + "m")
        o.tt("vector", C(3), C(2), C(2), ALU.mult, tag=tag + "m2")
        o.ts("vector", C(4), C(0), 1.0 / D, ALU.mult, tag=tag + "v")
        o.tt("vector", C(4), C(4), C(3), ALU.subtract, tag=tag + "v2")
        o.act(C(5), C(4), AF.Ln, bias=float(eps), tag=tag + "l")
        o.act(C(5), C(5), AF.Exp, scale=-0.5, tag=tag + "e")
        o.ts("vector", C(6), C(4), float(eps), ALU.add, tag=tag + "a")
        o.tt("vector", C(6), C(6), C(5), ALU.mult, tag=tag + "1")
        o.tt("vector", C(6), C(6), C(5), ALU.mult, tag=tag + "2")
        o.ts("vector", C(6), C(6), -0.5, ALU.mult, 1.5, ALU.add, tag=tag + "3")
        o.tt("vector", C(5), C(5), C(6), ALU.mult, tag=tag + "4")
        o.ts("vector", vw(dump), vw(xb), C(2), ALU.subtract, C(5), ALU.mult, tag=tag + "n")
        o.tt("vector", vw(dump), vw(dump), vw(gamma_b), ALU.mult, tag=tag + "g")
        o.tt("vector", vw(oo), vw(dump), vw(beta_b), ALU.add, tag=tag + "b")
        if dst_fm_sb is not None:
            emit_transpose_fm_from_tm(em, oo, dst_fm_sb, mk.ident, mk.ptrb, D,
                                      tok0=t * 128, tag=tag + "t")
        if dst_tm_scr is not None:
            em.dma("sync", dst_tm_scr.handle[t], oo.ap(), reads=[R(oo)],
                   writes=[R(dst_tm_scr, 0, 128, t * D, (t + 1) * D)], tag=tag + "s")


def emit_act_stationary(mk, lin, src_fm, dst_tm, t0g, w_bufs, n_tok=512, evac=None,
                        psum_pool=None, tag="as"):
    em, o = mk.em, mk.o
    kt, p_out, n_pan = lin.kt, lin.p_out, lin.n_pan
    pool = psum_pool if psum_pool is not None else mk.ps
    pi = 0
    for pan in range(n_pan):
        wbuf = w_bufs[pan % len(w_bufs)]
        em.dma("gpsimd", wbuf.handle[:, 0:kt * p_out], lin.w[pan],
               reads=[R(lin.w)], writes=[R(wbuf, 0, 128, 0, kt * p_out)],
               tag=tag + "wd")
        for lt in range(n_tok // 128):
            psb = pool[pi % len(pool)]
            pi += 1
            for k in range(kt):
                o.mm(V(psb, psb.handle[:, 0:p_out], (0, 128, 0, p_out)),
                     vw(src_fm, mt=k, l0=t0g + lt * 128, nl=128),
                     V(wbuf, wbuf.handle[:, k * p_out:(k + 1) * p_out],
                       (0, 128, k * p_out, (k + 1) * p_out)),
                     start=(k == 0), stop=(k == kt - 1), tag=tag + "mm")
            if evac is not None:
                evac(psb, lt, pan)
            else:
                dfree = dst_tm.shape[-1]
                o.act(V(dst_tm, dst_tm.handle[:, lt, pan * p_out:(pan + 1) * p_out],
                        (0, 128, lt * dfree + pan * p_out, lt * dfree + (pan + 1) * p_out)),
                      V(psb, psb.handle[:, 0:p_out], (0, 128, 0, p_out)),
                      AF.Copy, q="scalar", tag=tag + "ev")


def emit_transpose_tm_from_fm(mk, src_fm, tok0, dst_tm_buf, d=D, src_mt0=0, tag="t2m"):
    em, o = mk.em, mk.o
    for ct in range(d // 128):
        psb = mk.ptrb[ct % 2]
        o.transpose(V(psb, psb.handle[:, 0:128], (0, 128, 0, 128)),
                    vw(src_fm, mt=src_mt0 + ct, l0=tok0, nl=128), mk.ident, tag=tag + "T")
        o.copy("vector", vw(dst_tm_buf, l0=ct * 128, nl=128),
               V(psb, psb.handle[:, 0:128], (0, 128, 0, 128)), tag=tag + "E")


def stage_rwkv(mk, xn2_fm, W, s0_tm_scr, bw_mt, tag="rw", upto="full", dbg=None):
    em, o = mk.em, mk.o
    H, HS, CH, G = 16, 64, 64, 512
    CPG, NG = G // CH, T // G
    u_fm = em.sbuf(tag + "_u", [128, 8, G], BF16)
    kf = em.sbuf(tag + "_kf", [128, 8, G], BF16)
    rf = em.sbuf(tag + "_rf", [128, 8, G], BF16)
    vt = em.sbuf(tag + "_vt", [128, 4, 1024], BF16)
    lT = em.sbuf(tag + "_lT", [128, 8, 514], F32)
    spf = em.sbuf(tag + "_sp", [128, G], F32)
    ones = em.sbuf(tag + "_zz", [128, G], F32)
    fac = em.sbuf(tag + "_fa", [128, G], F32)
    fd1 = em.sbuf(tag + "_fd", [128, G], F32)
    rA = u_fm
    kB = em.sbuf(tag + "_kB", [128, 8, G], BF16)
    rz = kf
    ktil = em.sbuf(tag + "_kt", [128, 1024], BF16)
    v_dup = em.sbuf(tag + "_vd", [128, 8, 1024], BF16)
    dL = em.sbuf(tag + "_dL", [128, 8, 8], F32)
    dM = em.sbuf(tag + "_dM", [128, 8, 8], F32)
    S_sb = em.sbuf(tag + "_S", [128, 8, HS], F32)
    S_bf = em.sbuf(tag + "_Sb", [128, 8, HS], BF16)
    y_fm = em.sbuf(tag + "_y", [128, 8, G], BF16)
    P_m = em.sbuf(tag + "_P", [128, 512], BF16)
    s_out = em.sbuf(tag + "_so", [128, 8, G], BF16)
    s_tmb = [em.sbuf(f"{tag}_stm{i}", [128, D], BF16) for i in range(2)]
    clampP = em.sbuf(tag + "_cp", [128, 512], BF16)
    clampN = em.sbuf(tag + "_cn", [128, 512], BF16)
    wb = mk.wb
    pP = [mk.ps[0], mk.ps[1]]
    pY, pSd = mk.ps2[0], mk.ps2[1]
    BIG = 3.0e38

    o.memset("vector", vw(ones), 1.0, tag=tag + "z0")
    o.memset("vector", vw(clampP), BIG, tag=tag + "cp0")
    for half in range(2):
        cv = V(clampP, clampP.handle[half * 64:(half + 1) * 64, :],
               (half * 64, (half + 1) * 64, 0, 512))
        o.affine_select(cv, cv, [[0, 8], [1, 64]], ALU.is_ge, 0.0,
                        base=0, channel_multiplier=-1, tag=tag + "cpa")
    o.ts("vector", vw(clampN), vw(clampP), -1.0, ALU.mult, tag=tag + "cn")
    o.memset("vector", vw(S_sb), 0.0, tag=tag + "S0")
    o.memset("vector", vw(y_fm), 0.0, tag=tag + "y00")
    o.memset("vector", vw(P_m), 0.0, tag=tag + "P00")
    o.memset("vector", vw(ktil), 0.0, tag=tag + "kt00")
    o.memset("vector", vw(S_bf), 0.0, tag=tag + "Sb0")
    o.memset("vector", V(lT, lT.handle[:, :, 0:1], (0, 128, 0, 8 * 514)), 0.0,
             tag=tag + "l00")

    ngroups = NG if upto == "full" else 1
    for g in range(ngroups):
        t0g = g * G
        emit_linear(em, W["Wx"], xn2_fm, u_fm, G, wb, mk.ps, act=AF.Copy,
                    lg_size=512, tok0=0, src_tok0=t0g, tag=tag + "u")

        def z_evac(em_, psb, mt_g, mt_rel, lg, l0, nl):
            bi = V(bw_mt, bw_mt.handle[:, mt_g:mt_g + 1], (0, 128, mt_g, mt_g + 1))
            o.act(vw(spf, l0=0, nl=nl), vw(psb, l0=0, nl=nl), AF.Exp, bias=bi,
                  tag=tag + "ze")
            o.act(vw(spf, l0=0, nl=nl), vw(spf, l0=0, nl=nl), AF.Ln, bias=1.0,
                  tag=tag + "zl")
            o.scan_mult_add("vector",
                            V(lT, lT.handle[:, mt_g, 1:1 + nl],
                              (0, 128, mt_g * 514 + 1, mt_g * 514 + 1 + nl)),
                            vw(ones, l0=0, nl=nl), vw(spf, l0=0, nl=nl),
                            initial=0.0, tag=tag + "zs")
        emit_linear(em, W["Ww"], u_fm, spf, G, wb, mk.ps, evac_fn=z_evac,
                    lg_size=512, tok0=0, src_tok0=0, tag=tag + "z")
        emit_linear(em, W["Wk"], xn2_fm, kf, G, wb, mk.ps, act=AF.Copy,
                    lg_size=512, tok0=0, src_tok0=t0g, tag=tag + "k")
        emit_act_stationary(mk, W["Wv"], xn2_fm, vt, t0g, wb, tag=tag + "v")
        emit_linear(em, W["Wr"], xn2_fm, rf, G, wb, mk.ps, act=AF.Sigmoid,
                    lg_size=512, tok0=0, src_tok0=t0g, tag=tag + "r")
        if upto == "proj":
            if dbg is not None:
                for nm, b in [("kf", kf), ("rf", rf), ("vt", vt), ("lT", lT), ("u", u_fm)]:
                    em.dma("sync", dbg[nm].ap(), b.ap(), reads=[R(b)], writes=[R(dbg[nm])])
            return
        for ct in range(8):
            lreg = (0, 128, ct * 514, (ct + 1) * 514)
            lfull = lT.handle[:, ct, 1:513].rearrange("p (c l) -> p c l", c=8)
            bm = lT.handle[:, ct, 32:513:64].unsqueeze(2).broadcast_to([128, 8, 64])
            b0 = lT.handle[:, ct, 0:449:64].unsqueeze(2).broadcast_to([128, 8, 64])
            fdv = V(fd1, fd1.handle[:].rearrange("p (c l) -> p c l", c=8), (0, 128, 0, G))
            fav = V(fac, fac.handle[:].rearrange("p (c l) -> p c l", c=8), (0, 128, 0, G))
            o.tt("vector", fdv, V(lT, lfull, lreg), V(lT, bm, lreg), ALU.subtract,
                 tag=tag + "d1")
            o.act(vw(fac), vw(fd1), AF.Exp, scale=-1.0, tag=tag + "eA")
            o.tt("vector", vw(rA, mt=ct), vw(rf, mt=ct), vw(fac), ALU.mult, tag=tag + "rA")
            o.act(vw(fac), vw(fd1), AF.Exp, tag=tag + "eB")
            o.tt("vector", vw(kB, mt=ct), vw(kf, mt=ct), vw(fac), ALU.mult, tag=tag + "kB")
            o.tt("vector", fav, V(lT, lfull, lreg), V(lT, b0, lreg), ALU.subtract,
                 tag=tag + "d0")
            o.act(vw(fac), vw(fac), AF.Exp, scale=-1.0, tag=tag + "e0")
            o.tt("vector", vw(rz, mt=ct), vw(rf, mt=ct), vw(fac), ALU.mult, tag=tag + "rz")
            dLv = V(dL, dL.handle[:, ct, :], (0, 128, ct * 8, (ct + 1) * 8))
            dMv = V(dM, dM.handle[:, ct, :], (0, 128, ct * 8, (ct + 1) * 8))
            ends = V(lT, lT.handle[:, ct, 64:513:64], lreg)
            starts = V(lT, lT.handle[:, ct, 0:449:64], lreg)
            mids = V(lT, lT.handle[:, ct, 32:513:64], lreg)
            o.tt("vector", dLv, ends, starts, ALU.subtract, tag=tag + "dl")
            o.act(dLv, dLv, AF.Exp, scale=-1.0, tag=tag + "dle")
            o.tt("vector", dMv, ends, mids, ALU.subtract, tag=tag + "dm")
            o.act(dMv, dMv, AF.Exp, scale=-1.0, tag=tag + "dme")
        if upto == "factors":
            if dbg is not None:
                for nm, b in [("rA", rA), ("kB", kB), ("rz", rz), ("dL", dL), ("dM", dM), ("lT", lT)]:
                    em.dma("sync", dbg[nm].ap(), b.ap(), reads=[R(b)], writes=[R(dbg[nm])])
            return
        for c in range(CPG):
            pb = (c % 2) * 64
            src = V(vt, vt.handle[pb:pb + 64, c // 2, :],
                    (pb, pb + 64, (c // 2) * 1024, (c // 2 + 1) * 1024))
            for half in range(2):
                o.copy("vector",
                       V(v_dup, v_dup.handle[half * 64:half * 64 + 64, c, :],
                         (half * 64, half * 64 + 64, c * 1024, (c + 1) * 1024)),
                       src, tag=tag + "vd")
            for ct in range(8):
                psb = mk.ptrb[ct % 2]
                o.transpose(V(psb, psb.handle[0:64, 0:128], (0, 64, 0, 128)),
                            V(kB, kB.handle[:, ct, c * 64:(c + 1) * 64],
                              (0, 128, ct * G + c * 64, ct * G + (c + 1) * 64)),
                            mk.ident, tag=tag + "tT")
                o.copy("vector",
                       V(ktil, ktil.handle[0:64, ct * 128:(ct + 1) * 128],
                         (0, 64, ct * 128, (ct + 1) * 128)),
                       V(psb, psb.handle[0:64, 0:128], (0, 64, 0, 128)), tag=tag + "tE")
            o.copy("vector", V(ktil, ktil.handle[64:128, :], (64, 128, 0, 1024)),
                   V(ktil, ktil.handle[0:64, :], (0, 64, 0, 1024)), tag=tag + "tD")
            for h in range(H):
                hp, ct = (h % 2) * 64, h // 2
                o.mm(V(pP[0], pP[0].handle[hp:hp + 64, ct * 64:(ct + 1) * 64],
                       (hp, hp + 64, ct * 64, (ct + 1) * 64)),
                     V(kB, kB.handle[hp:hp + 64, ct, c * 64:(c + 1) * 64],
                       (hp, hp + 64, ct * G + c * 64, ct * G + (c + 1) * 64)),
                     V(rA, rA.handle[hp:hp + 64, ct, c * 64:(c + 1) * 64],
                       (hp, hp + 64, ct * G + c * 64, ct * G + (c + 1) * 64)),
                     tile_position=(hp, hp), tag=tag + "P")
            o.act(vw(P_m), V(pP[0], pP[0].handle[:], (0, 128, 0, 512)),
                  AF.Copy, q="scalar", tag=tag + "Pe")
            o.tt("vector", vw(P_m), vw(P_m), vw(clampP), ALU.min, tag=tag + "c1")
            o.tt("vector", vw(P_m), vw(P_m), vw(clampN), ALU.max, tag=tag + "c2")
            if upto == "pmm" and c == 0:
                if dbg is not None:
                    em.dma("sync", dbg["P"].ap(), P_m.ap(), reads=[R(P_m)], writes=[R(dbg["P"])])
                    em.dma("sync", dbg["cp"].ap(), clampP.ap(), reads=[R(clampP)], writes=[R(dbg["cp"])])
                return
            for h in range(H):
                hp, ct = (h % 2) * 64, h // 2
                yv = V(pY, pY.handle[hp:hp + 64, ct * 64:(ct + 1) * 64],
                       (hp, hp + 64, ct * 64, (ct + 1) * 64))
                o.mm(yv,
                     V(v_dup, v_dup.handle[hp:hp + 64, c, h * 64:(h + 1) * 64],
                       (hp, hp + 64, c * 1024 + h * 64, c * 1024 + (h + 1) * 64)),
                     V(P_m, P_m.handle[hp:hp + 64, ct * 64:(ct + 1) * 64],
                       (hp, hp + 64, ct * 64, (ct + 1) * 64)),
                     start=True, stop=False, tile_position=(hp, hp), tag=tag + "av")
                o.mm(yv,
                     V(S_bf, S_bf.handle[hp:hp + 64, ct, :],
                       (hp, hp + 64, ct * HS, (ct + 1) * HS)),
                     V(rz, rz.handle[hp:hp + 64, ct, c * 64:(c + 1) * 64],
                       (hp, hp + 64, ct * G + c * 64, ct * G + (c + 1) * 64)),
                     start=False, stop=True, tile_position=(hp, hp), tag=tag + "in")
                sv = V(pSd, pSd.handle[hp:hp + 64, ct * 64:(ct + 1) * 64],
                       (hp, hp + 64, ct * 64, (ct + 1) * 64))
                o.mm(sv,
                     V(ktil, ktil.handle[hp:hp + 64, h * 64:(h + 1) * 64],
                       (hp, hp + 64, h * 64, (h + 1) * 64)),
                     V(v_dup, v_dup.handle[hp:hp + 64, c, h * 64:(h + 1) * 64],
                       (hp, hp + 64, c * 1024 + h * 64, c * 1024 + (h + 1) * 64)),
                     start=True, stop=True, tile_position=(hp, hp), tag=tag + "Sd")
            if upto == "av1" and c == 0:
                return
            o.copy("scalar",
                   V(y_fm, y_fm.handle[:, :, c * 64:(c + 1) * 64],
                     (0, 128, 0, 8 * G)),
                   V(pY, pY.handle[:].rearrange("p (a l) -> p a l", a=8), (0, 128, 0, 512)),
                   tag=tag + "ye")
            if upto == "yev" and c == 0:
                return
            dLb = V(dL, dL.handle[:, :, c:c + 1].broadcast_to([128, 8, HS]),
                    (0, 128, 0, 64))
            dMb = V(dM, dM.handle[:, :, c:c + 1].broadcast_to([128, 8, HS]),
                    (0, 128, 0, 64))
            Sv3 = V(S_sb, S_sb.handle[:], (0, 128, 0, 8 * HS))
            tmp3 = V(mk.dump, mk.dump.handle[:, 0:512].rearrange(
                "p (a b) -> p a b", a=8), (0, 128, 0, 512))
            pSd3 = V(pSd, pSd.handle[:].rearrange("p (a b) -> p a b", a=8),
                     (0, 128, 0, 512))
            o.tt("vector", Sv3, Sv3, dLb, ALU.mult, tag=tag + "s1")
            o.tt("vector", tmp3, pSd3, dMb, ALU.mult, tag=tag + "s2a")
            o.tt("vector", Sv3, Sv3, tmp3, ALU.add, tag=tag + "s2b")
            if upto == "supd" and c == 0:
                return
            o.copy("scalar", vw(S_bf), vw(S_sb), tag=tag + "sbf")
            if upto == "chunk1" and c == 0:
                if dbg is not None:
                    for nm, b_ in [("P", P_m), ("y", y_fm), ("S", S_sb), ("ktil", ktil),
                                   ("vt", vt), ("dM", dM), ("dL", dL)]:
                        em.dma("sync", dbg[nm].ap(), b_.ap(), reads=[R(b_)], writes=[R(dbg[nm])])
                return
        if upto == "chunks":
            if dbg is not None:
                for nm, b_ in [("P", P_m), ("y", y_fm), ("S", S_sb)]:
                    em.dma("sync", dbg[nm].ap(), b_.ap(), reads=[R(b_)], writes=[R(dbg[nm])])
            return
        emit_linear(em, W["Wo"], y_fm, s_out, G, wb, mk.ps, act=AF.Copy,
                    lg_size=512, tok0=0, src_tok0=0, tag=tag + "o")
        for lt in range(4):
            tb = s_tmb[lt % 2]
            emit_transpose_tm_from_fm(mk, s_out, lt * 128, tb, tag=tag + "ot")
            em.dma("sync", s0_tm_scr.handle[g * 4 + lt], tb.ap(), reads=[R(tb)],
                   writes=[R(s0_tm_scr, 0, 128, (g * 4 + lt) * D, (g * 4 + lt + 1) * D)],
                   tag=tag + "os")



def stage_l1attn(mk, ln1_fm, xn1_tm_scr, W, vecs, add1_tm_scr, attn_tm_scr,
                 s1_tm_scr, eps_ln=1e-6, tag="la", upto="full", dbg=None):
    em, o = mk.em, mk.o
    H, DH = 16, 64
    kp_tm = em.sbuf(tag + "_kp", [128, 1024], BF16)
    v_aug = em.sbuf(tag + "_va", [128, 16, 65], BF16)
    kvs = em.sbuf(tag + "_kvs", [128, 8, 65], F32)
    kvs_pad = em.sbuf(tag + "_kvb", [128, 16, 65], BF16)
    qp_g = em.sbuf(tag + "_qp", [128, 8, 512], BF16)
    att_tm = em.sbuf(tag + "_at", [128, 1024], BF16)
    att_fm = em.sbuf(tag + "_af", [128, 8, 512], BF16)
    den16 = em.sbuf(tag + "_dn", [128, 16], F32)
    rec16 = em.sbuf(tag + "_rc", [128, 16], F32)
    tmp16 = em.sbuf(tag + "_tm16", [128, 16], F32)
    ao_fm = qp_g
    t_a = [em.sbuf(f"{tag}_ta{i}", [128, D], BF16) for i in range(2)]
    t_x = [em.sbuf(f"{tag}_tx0", [128, D], F32)] * 2
    t_o = [em.sbuf(f"{tag}_to{i}", [128, D], BF16) for i in range(2)]
    dump = em.sbuf(tag + "_dp", [128, D], F32)
    st = em.sbuf(tag + "_st", [128, 16], F32)
    wb = mk.wb
    pk = [mk.ps2[0], mk.ps2[1]]

    o.memset("vector", V(v_aug, v_aug.handle[:, :, 64:65], (0, 128, 0, 16 * 65)), 1.0,
             tag=tag + "on")
    o.memset("vector", vw(kvs), 0.0, tag=tag + "k0")

    for pan in range(2):
        em.dma("gpsimd", wb[0].handle[:, pan * 4096:(pan + 1) * 4096], W["wk"].w[pan],
               reads=[R(W["wk"].w)], writes=[R(wb[0], 0, 128, pan * 4096, (pan + 1) * 4096)],
               tag=tag + "wk")
        em.dma("gpsimd", wb[1].handle[:, pan * 4096:(pan + 1) * 4096], W["wv"].w[pan],
               reads=[R(W["wv"].w)], writes=[R(wb[1], 0, 128, pan * 4096, (pan + 1) * 4096)],
               tag=tag + "wv")
    for lt in range(NT):
        for pan in range(2):
            for which, wbuf, evac in ((0, wb[0], "kp"), (1, wb[1], "va")):
                psb = mk.ps[(lt * 4 + pan * 2 + which) % 4]
                for k in range(8):
                    o.mm(V(psb, psb.handle[:, 0:512], (0, 128, 0, 512)),
                         vw(ln1_fm, mt=k, l0=lt * 128, nl=128),
                         V(wbuf, wbuf.handle[:, pan * 4096 + k * 512:pan * 4096 + (k + 1) * 512],
                           (0, 128, pan * 4096 + k * 512, pan * 4096 + (k + 1) * 512)),
                         start=(k == 0), stop=(k == 7), tag=tag + "pj")
                if which == 0:
                    o.ts("vector", V(kp_tm, kp_tm.handle[:, pan * 512:(pan + 1) * 512],
                                     (0, 128, pan * 512, (pan + 1) * 512)),
                         V(psb, psb.handle[:, 0:512], (0, 128, 0, 512)),
                         0.0, ALU.max, 1e-3, ALU.add, tag=tag + "rk")
                else:
                    o.copy("scalar",
                           V(v_aug, v_aug.handle[:, pan * 8:(pan + 1) * 8, 0:64],
                             (0, 128, pan * 8 * 65, (pan + 1) * 8 * 65)),
                           V(psb, psb.handle[:].rearrange("p (h d) -> p h d", h=8),
                             (0, 128, 0, 512)), tag=tag + "ve")
        for h in range(H):
            hp = (h % 2) * 64
            bank = pk[h // 8]
            j = (h % 8) // 2
            o.mm(V(bank, bank.handle[hp:hp + 64, j * 65:(j + 1) * 65],
                   (hp, hp + 64, j * 65, (j + 1) * 65)),
                 V(kp_tm, kp_tm.handle[:, h * 64:(h + 1) * 64], (0, 128, h * 64, (h + 1) * 64)),
                 V(v_aug, v_aug.handle[:, h, :], (0, 128, h * 65, (h + 1) * 65)),
                 tile_position=(0, hp), tag=tag + "kv")
        for b in range(2):
            o.tt("vector", V(kvs, kvs.handle[:, b * 4:(b + 1) * 4, :],
                             (0, 128, b * 4 * 65, (b + 1) * 4 * 65)),
                 V(kvs, kvs.handle[:, b * 4:(b + 1) * 4, :],
                   (0, 128, b * 4 * 65, (b + 1) * 4 * 65)),
                 V(pk[b], pk[b].handle[:, 0:260].rearrange("p (j e) -> p j e", j=4),
                   (0, 128, 0, 512)),
                 ALU.add, tag=tag + "ka")
    if dbg is not None:
        em.dma("sync", dbg["kvs"].ap(), kvs.ap(), reads=[R(kvs)], writes=[R(dbg["kvs"])])
        em.dma("sync", dbg["vaug"].ap(), v_aug.ap(), reads=[R(v_aug)], writes=[R(dbg["vaug"])])
        em.dma("sync", dbg["kp"].ap(), kp_tm.ap(), reads=[R(kp_tm)], writes=[R(dbg["kp"])])
    o.memset("vector", vw(kvs_pad), 0.0, tag=tag + "kp0")
    for h in range(H):
        hp = (h % 2) * 64
        sl = h // 2
        o.copy("vector",
               V(kvs_pad, kvs_pad.handle[hp:hp + 64, h, :], (hp, hp + 64, h * 65, (h + 1) * 65)),
               V(kvs, kvs.handle[hp:hp + 64, sl, :], (hp, hp + 64, sl * 65, (sl + 1) * 65)),
               tag=tag + "kc")

    for pan in range(2):
        em.dma("gpsimd", wb[0].handle[:, pan * 4096:(pan + 1) * 4096], W["wq"].w[pan],
               reads=[R(W["wq"].w)], writes=[R(wb[0], 0, 128, pan * 4096, (pan + 1) * 4096)],
               tag=tag + "wq")
        em.dma("gpsimd", wb[1].handle[:, pan * 4096:(pan + 1) * 4096], W["wo"].w[pan],
               reads=[R(W["wo"].w)], writes=[R(wb[1], 0, 128, pan * 4096, (pan + 1) * 4096)],
               tag=tag + "wo")
    for g in range(T // 512):
        t0g = g * 512
        for pan in range(2):
            for m in range(4):
                mt = pan * 4 + m
                psb = mk.ps[mt % 4]
                for k in range(8):
                    o.mm(V(psb, psb.handle[:, 0:512], (0, 128, 0, 512)),
                         V(wb[0], wb[0].handle[:, pan * 4096 + k * 512 + m * 128:
                                               pan * 4096 + k * 512 + (m + 1) * 128],
                           (0, 128, pan * 4096 + k * 512 + m * 128,
                            pan * 4096 + k * 512 + (m + 1) * 128)),
                         vw(ln1_fm, mt=k, l0=t0g, nl=512),
                         start=(k == 0), stop=(k == 7), tag=tag + "qm")
                o.ts("vector", vw(qp_g, mt=mt, l0=0, nl=512),
                     V(psb, psb.handle[:, 0:512], (0, 128, 0, 512)),
                     0.0, ALU.max, 1e-3, ALU.add, tag=tag + "qr")
        for lt in range(4):
            for h in range(H):
                bank = mk.ps[h // 4]
                col = (h % 4) // 2 * 130 + (h % 2) * 65
                o.mm(V(bank, bank.handle[:, col:col + 65], (0, 128, col, col + 65)),
                     V(qp_g, qp_g.handle[:, h // 2, lt * 128:(lt + 1) * 128],
                       (0, 128, (h // 2) * 512 + lt * 128, (h // 2) * 512 + (lt + 1) * 128)),
                     V(kvs_pad, kvs_pad.handle[:, h, :], (0, 128, h * 65, (h + 1) * 65)),
                     tag=tag + "nm")
            for h in range(H):
                bank = mk.ps[h // 4]
                col = (h % 4) // 2 * 130 + (h % 2) * 65 + 64
                o.copy("vector", V(den16, den16.handle[:, h:h + 1], (0, 128, h, h + 1)),
                       V(bank, bank.handle[:, col:col + 1], (0, 128, col, col + 1)),
                       tag=tag + "dc")
            o.act(vw(rec16), vw(den16), AF.Ln, tag=tag + "rl")
            o.act(vw(rec16), vw(rec16), AF.Exp, scale=-1.0, tag=tag + "re")
            o.tt("vector", vw(tmp16), vw(den16), vw(rec16), ALU.mult, tag=tag + "r1")
            o.ts("vector", vw(tmp16), vw(tmp16), -1.0, ALU.mult, 2.0, ALU.add, tag=tag + "r2")
            o.tt("vector", vw(rec16), vw(rec16), vw(tmp16), ALU.mult, tag=tag + "r3")
            for h in range(H):
                bank = mk.ps[h // 4]
                col = (h % 4) // 2 * 130 + (h % 2) * 65
                o.act(V(att_tm, att_tm.handle[:, h * 64:(h + 1) * 64],
                        (0, 128, h * 64, (h + 1) * 64)),
                      V(bank, bank.handle[:, col:col + 64], (0, 128, col, col + 64)),
                      AF.Copy, scale=V(rec16, rec16.handle[:, h:h + 1], (0, 128, h, h + 1)),
                      q="scalar", tag=tag + "ae")
            if dbg is not None and g == 0 and lt == 0:
                em.dma("sync", dbg["den"].ap(), den16.ap(), reads=[R(den16)], writes=[R(dbg["den"])])
                em.dma("sync", dbg["rec"].ap(), rec16.ap(), reads=[R(rec16)], writes=[R(dbg["rec"])])
                em.dma("sync", dbg["qp"].ap(), qp_g.ap(), reads=[R(qp_g)], writes=[R(dbg["qp"])])
                em.dma("sync", dbg["attm"].ap(), att_tm.ap(), reads=[R(att_tm)], writes=[R(dbg["attm"])])
            emit_transpose_fm_from_tm(em, att_tm, att_fm, mk.ident, mk.ptrb, D,
                                      tok0=lt * 128, tag=tag + "atr")
        for pan in range(2):
            for m in range(4):
                mt = pan * 4 + m
                psb = mk.ps[mt % 4]
                for k in range(8):
                    o.mm(V(psb, psb.handle[:, 0:512], (0, 128, 0, 512)),
                         V(wb[1], wb[1].handle[:, pan * 4096 + k * 512 + m * 128:
                                               pan * 4096 + k * 512 + (m + 1) * 128],
                           (0, 128, pan * 4096 + k * 512 + m * 128,
                            pan * 4096 + k * 512 + (m + 1) * 128)),
                         vw(att_fm, mt=k, l0=0, nl=512),
                         start=(k == 0), stop=(k == 7), tag=tag + "om")
                o.copy("scalar", vw(ao_fm, mt=mt, l0=0, nl=512),
                       V(psb, psb.handle[:, 0:512], (0, 128, 0, 512)), tag=tag + "oe")
        for lt in range(4):
            tb = t_a[lt % 2]
            emit_transpose_tm_from_fm(mk, ao_fm, lt * 128, tb, tag=tag + "aot")
            em.dma("sync", attn_tm_scr.handle[g * 4 + lt], tb.ap(), reads=[R(tb)],
                   writes=[R(attn_tm_scr, 0, 128, (g * 4 + lt) * D, (g * 4 + lt + 1) * D)],
                   tag=tag + "aos")

    if upto == "p2":
        return
    ln2_fm = ln1_fm
    for t in range(NT):
        ta, tx, to = t_a[t % 2], t_x[t % 2], t_o[t % 2]
        em.dma("gpsimd", tx.ap(), attn_tm_scr.handle[t], reads=[R(attn_tm_scr)],
               writes=[R(tx)], tag=tag + "l3a")
        em.dma("sync", ta.ap(), xn1_tm_scr.handle[t], reads=[R(xn1_tm_scr)],
               writes=[R(ta)], tag=tag + "l3x")
        o.tt("vector", vw(tx), vw(tx), vw(ta), ALU.add, tag=tag + "a1")
        o.copy("vector", vw(to), vw(tx), tag=tag + "a1c")
        em.dma("sync", add1_tm_scr.handle[t], to.ap(), reads=[R(to)],
               writes=[R(add1_tm_scr, 0, 128, t * D, (t + 1) * D)], tag=tag + "a1s")
        C = lambda j: V(st, st.handle[:, j:j + 1], (0, 128, j, j + 1))
        o.act(vw(dump), vw(tx), AF.Square, accum=C(0), tag=tag + "q2")
        o.reduce("vector", C(1), vw(tx), ALU.add, tag=tag + "r2_")
        o.ts("vector", C(2), C(1), 1.0 / D, ALU.mult, tag=tag + "m_")
        o.tt("vector", C(3), C(2), C(2), ALU.mult, tag=tag + "m2_")
        o.ts("vector", C(4), C(0), 1.0 / D, ALU.mult, tag=tag + "v_")
        o.tt("vector", C(4), C(4), C(3), ALU.subtract, tag=tag + "v2_")
        o.act(C(5), C(4), AF.Ln, bias=float(eps_ln), tag=tag + "l_")
        o.act(C(5), C(5), AF.Exp, scale=-0.5, tag=tag + "e_")
        o.ts("vector", C(6), C(4), float(eps_ln), ALU.add, tag=tag + "a_")
        o.tt("vector", C(6), C(6), C(5), ALU.mult, tag=tag + "1_")
        o.tt("vector", C(6), C(6), C(5), ALU.mult, tag=tag + "2_")
        o.ts("vector", C(6), C(6), -0.5, ALU.mult, 1.5, ALU.add, tag=tag + "3_")
        o.tt("vector", C(5), C(5), C(6), ALU.mult, tag=tag + "4_")
        o.ts("vector", vw(dump), vw(tx), C(2), ALU.subtract, C(5), ALU.mult, tag=tag + "n_")
        o.tt("vector", vw(dump), vw(dump), vw(vecs["gamma1_b"]), ALU.mult, tag=tag + "g_")
        o.tt("vector", vw(to), vw(dump), vw(vecs["beta1_b"]), ALU.add, tag=tag + "b_")
        emit_transpose_fm_from_tm(em, to, ln2_fm, mk.ident, mk.ptrb, D, tok0=t * 128,
                                  tag=tag + "t_")

    if upto == "p3":
        return
    hbuf = em.sbuf(tag + "_h", [128, 32, 512], BF16)
    o2_fm = att_fm
    for tg in range(T // 512):
        t0g = tg * 512
        emit_linear(em, W["w1"], ln2_fm, hbuf, 512, wb, mk.ps, act=AF.Gelu,
                    bias=vecs["b1_mt"], lg_size=512, tok0=0, src_tok0=t0g, tag=tag + "f1")
        if True:
            emit_linear(em, W["w2"], hbuf, o2_fm, 512, wb, mk.ps, act=AF.Gelu,
                        bias=vecs["b2_mt"], lg_size=512, tok0=0, src_tok0=0,
                        tag=tag + "f2")
            for lt in range(4):
                gt = tg * 4 + lt
                tb = t_o[lt % 2]
                emit_transpose_tm_from_fm(mk, o2_fm, lt * 128, tb, tag=tag + "f2t")
                ta = t_a[lt % 2]
                em.dma("sync", ta.ap(), add1_tm_scr.handle[gt], reads=[R(add1_tm_scr)],
                       writes=[R(ta)], tag=tag + "f2l")
                o.tt("vector", vw(tb), vw(tb), vw(ta), ALU.add, tag=tag + "f2a")
                em.dma("sync", s1_tm_scr.handle[gt], tb.ap(), reads=[R(tb)],
                       writes=[R(s1_tm_scr, 0, 128, gt * D, (gt + 1) * D)], tag=tag + "f2s")



def stage_light(mk, xn0_tm_scr, lnL_fm, W, vecs, light_tm_scr, eps=1e-5, tag="lg", upto="full", dbg=None):
    em, o = mk.em, mk.o
    H, DH, GRP = 8, 64, 1024
    NG = T // GRP
    stage_ln_affine(mk, xn0_tm_scr, lnL_fm, vecs["lgamma_b"], vecs["lbeta_b"], eps,
                    tag=tag + "ln")
    q_fm = em.sbuf(tag + "_q", [128, 4, GRP], BF16)
    k_fm = em.sbuf(tag + "_k", [128, 4, GRP + 256], BF16)
    v_tm = em.sbuf(tag + "_v", [128, GRP // 128 + 2, 512], BF16)
    p_sb = em.sbuf(tag + "_p", [128, 6, 512], BF16)
    dtmp = mk.dump
    rec_sb = mk.tf32[0]
    maskL = em.sbuf(tag + "_mL", [128, 128], BF16)
    maskR = em.sbuf(tag + "_mR", [128, 128], BF16)
    maskZ = em.sbuf(tag + "_mZ", [128, 128], BF16)
    lt_fm = lnL_fm
    t_o = mk.tbf
    wb = mk.wb
    wb3 = em.sbuf(tag + "_w3", [128, 4096], BF16)
    wb4 = [em.sbuf(f"{tag}_w4{i}", [128, 2048], BF16) for i in range(2)]
    onesrow = em.sbuf(tag + "_o1", [1, 128], F32)
    onescol = em.sbuf(tag + "_oc", [128, 1], BF16)
    halo_sb = em.sbuf(tag + "_halo", [128, 8, 128], BF16)

    o.memset("vector", V(onesrow, onesrow.handle[:], (0, 1, 0, 128)), 1.0, tag=tag + "or")
    o.memset("vector", V(onescol, onescol.handle[:], (0, 128, 0, 1)), 1.0, tag=tag + "ocm")
    o.memset("vector", vw(maskZ), 0.0, tag=tag + "mz")
    o.memset("vector", vw(maskL), 1.0, tag=tag + "ml0")
    o.memset("vector", V(maskL, maskL.handle[:, 64:128], (0, 128, 64, 128)), 0.0,
             tag=tag + "ml1")
    o.memset("vector", V(maskL, maskL.handle[0:64, 0:64], (0, 64, 0, 64)), 0.0,
             tag=tag + "ml2")
    o.memset("vector", vw(maskR), 1.0, tag=tag + "mr0")
    o.memset("vector", V(maskR, maskR.handle[:, 0:64], (0, 128, 0, 64)), 0.0,
             tag=tag + "mr1")
    o.memset("vector", V(maskR, maskR.handle[64:128, 64:128], (64, 128, 64, 128)), 0.0,
             tag=tag + "mr2")

    for pan, dst in ((0, wb[0]), (1, wb[1]), (2, wb3)):
        em.dma("gpsimd", dst.handle[:, 0:4096], W["qkv"].w[pan],
               reads=[R(W["qkv"].w)], writes=[R(dst, 0, 128, 0, 4096)], tag=tag + "wl")

    banks6 = [mk.ps[0], mk.ps[1], mk.ps[2], mk.ps[3], mk.ps2[0], mk.ps2[1]]

    def project(dst, wbuf, tok0_global, n_tok, dst_off, scale, tm=False):
        for lg in range(n_tok // 512):
            tg = tok0_global + lg * 512
            if tm:
                for lt in range(4):
                    psb = mk.ps[lt % 4]
                    for kk in range(8):
                        o.mm(V(psb, psb.handle[:, 0:512], (0, 128, 0, 512)),
                             vw(lnL_fm, mt=kk, l0=tg + lt * 128, nl=128),
                             V(wbuf, wbuf.handle[:, kk * 512:(kk + 1) * 512],
                               (0, 128, kk * 512, (kk + 1) * 512)),
                             start=(kk == 0), stop=(kk == 7), tag=tag + "pv")
                    ti = dst_off // 128 + lg * 4 + lt
                    o.copy("scalar", V(dst, dst.handle[:, ti, :],
                                       (0, 128, ti * 512, (ti + 1) * 512)),
                           V(psb, psb.handle[:, 0:512], (0, 128, 0, 512)), tag=tag + "pe")
            else:
                for m in range(4):
                    psb = mk.ps[m % 4]
                    for kk in range(8):
                        o.mm(V(psb, psb.handle[:, 0:512], (0, 128, 0, 512)),
                             V(wbuf, wbuf.handle[:, kk * 512 + m * 128:kk * 512 + (m + 1) * 128],
                               (0, 128, kk * 512 + m * 128, kk * 512 + (m + 1) * 128)),
                             vw(lnL_fm, mt=kk, l0=tg, nl=512),
                             start=(kk == 0), stop=(kk == 7), tag=tag + "pq")
                    L = dst.shape[-1]
                    o.act(V(dst, dst.handle[:, m, dst_off + lg * 512:dst_off + (lg + 1) * 512],
                            (0, 128, m * L + dst_off + lg * 512, m * L + dst_off + (lg + 1) * 512)),
                          V(psb, psb.handle[:, 0:512], (0, 128, 0, 512)),
                          AF.Copy, scale=scale, q="scalar", tag=tag + "pqe")

    for g in range(NG):
        t0g = g * GRP
        em.fence("tensor")
        project(q_fm, wb[0], t0g, GRP, 0, 0.125)
        lo = t0g - 128
        hi = t0g + GRP + 128
        lo_c = max(lo, 0)
        hi_c = min(hi, T)
        span = hi_c - lo_c
        L = GRP + 256
        if g == 1:
            for m in range(4):
                psb = mk.ps[m % 4]
                for kk in range(8):
                    o.mm(V(psb, psb.handle[:, 0:128], (0, 128, 0, 512)),
                         V(wb[1], wb[1].handle[:, kk * 512 + m * 128:kk * 512 + (m + 1) * 128],
                           (0, 128, kk * 512 + m * 128, kk * 512 + (m + 1) * 128)),
                         V(halo_sb, halo_sb.handle[:, kk, :], (0, 128, kk * 128, (kk + 1) * 128)),
                         start=(kk == 0), stop=(kk == 7), tag=tag + "pkh")
                o.copy("scalar", V(k_fm, k_fm.handle[:, m, 0:128], (0, 128, m * L, m * L + 128)),
                       V(psb, psb.handle[:, 0:128], (0, 128, 0, 512)), tag=tag + "pkhe")
            offs = [t0g, t0g + 512]
            if offs[-1] + 512 < hi_c:
                offs.append(hi_c - 512)
        else:
            offs = list(range(lo_c, hi_c - 511, 512))
            if offs and offs[-1] + 512 < hi_c:
                offs.append(hi_c - 512)
            elif not offs:
                offs = [lo_c]
        for tg in offs:
            for m in range(4):
                psb = mk.ps[m % 4]
                for kk in range(8):
                    o.mm(V(psb, psb.handle[:, 0:512], (0, 128, 0, 512)),
                         V(wb[1], wb[1].handle[:, kk * 512 + m * 128:kk * 512 + (m + 1) * 128],
                           (0, 128, kk * 512 + m * 128, kk * 512 + (m + 1) * 128)),
                         vw(lnL_fm, mt=kk, l0=tg, nl=512),
                         start=(kk == 0), stop=(kk == 7), tag=tag + "pk")
                dof = tg - lo
                L = GRP + 256
                o.copy("scalar",
                       V(k_fm, k_fm.handle[:, m, dof:dof + 512],
                         (0, 128, m * L + dof, m * L + dof + 512)),
                       V(psb, psb.handle[:, 0:512], (0, 128, 0, 512)), tag=tag + "pke")
        if lo < 0:
            o.memset("vector", V(k_fm, k_fm.handle[:, :, 0:128], (0, 128, 0, 4 * (GRP + 256))),
                     0.0, tag=tag + "kzl")
        if hi > T:
            o.memset("vector", V(k_fm, k_fm.handle[:, :, GRP + 128:GRP + 256],
                                 (0, 128, 0, 4 * (GRP + 256))), 0.0, tag=tag + "kzr")
        vlo = lo_c
        for ti in range((hi_c - vlo) // 128):
            tglob = vlo + ti * 128
            psb = mk.ps[ti % 4]
            for kk in range(8):
                lhs = (V(halo_sb, halo_sb.handle[:, kk, :], (0, 128, kk * 128, (kk + 1) * 128))
                       if (g == 1 and tglob < t0g) else
                       vw(lnL_fm, mt=kk, l0=tglob, nl=128))
                o.mm(V(psb, psb.handle[:, 0:512], (0, 128, 0, 512)),
                     lhs,
                     V(wb3, wb3.handle[:, kk * 512:(kk + 1) * 512],
                       (0, 128, kk * 512, (kk + 1) * 512)),
                     start=(kk == 0), stop=(kk == 7), tag=tag + "pv")
            dti = (tglob - lo) // 128
            o.copy("scalar", V(v_tm, v_tm.handle[:, dti, :], (0, 128, dti * 512, (dti + 1) * 512)),
                   V(psb, psb.handle[:, 0:512], (0, 128, 0, 512)), tag=tag + "pve")
        if lo < 0:
            o.memset("vector", V(v_tm, v_tm.handle[:, 0, :], (0, 128, 0, 512)), 0.0,
                     tag=tag + "vz0")
        if hi > T:
            o.memset("vector", V(v_tm, v_tm.handle[:, GRP // 128 + 1, :],
                                 (0, 128, (GRP // 128 + 1) * 512, (GRP // 128 + 2) * 512)),
                     0.0, tag=tag + "vz1")
        if g == 0:
            o.copy("vector", V(halo_sb, halo_sb.handle[:], (0, 128, 0, 8 * 128)),
                   V(lnL_fm, lnL_fm.handle[:, 0:8, GRP - 128:GRP], (0, 128, 0, 8 * T)),
                   tag=tag + "hsv")
        if upto == "vproj" and g == 0:
            return
        for qbl in range(GRP // 128):
            qb = g * (GRP // 128) + qbl
            em.fence("tensor")
            for par in range(2):
                hp = par * 64
                for j in range(3):
                    bank = banks6[par * 3 + j]
                    for hh in range(4):
                        h = hh * 2 + par
                        ct = h // 2
                        o.mm(V(bank, bank.handle[:, hh * 128:(hh + 1) * 128],
                               (0, 128, hh * 128, (hh + 1) * 128)),
                             V(k_fm, k_fm.handle[hp:hp + 64, ct, (qbl + j) * 128:(qbl + j + 1) * 128],
                               (hp, hp + 64, ct * (GRP + 256) + (qbl + j) * 128,
                                ct * (GRP + 256) + (qbl + j + 1) * 128)),
                             V(q_fm, q_fm.handle[hp:hp + 64, ct, qbl * 128:(qbl + 1) * 128],
                               (hp, hp + 64, ct * GRP + qbl * 128, ct * GRP + (qbl + 1) * 128)),
                             tag=tag + "sm")
                em.fence("tensor")
            for par in range(2):
                for j in range(3):
                    bank = banks6[par * 3 + j]
                    pv = V(p_sb, p_sb.handle[:, par * 3 + j, :],
                           (0, 128, (par * 3 + j) * 512, (par * 3 + j + 1) * 512))
                    o.act(pv, V(bank, bank.handle[:, 0:512], (0, 128, 0, 512)), AF.Exp,
                          q="scalar", tag=tag + "ex")
                    m = None
                    if j == 0:
                        m = maskZ if qb == 0 else maskL
                    elif j == 2:
                        m = maskZ if qb == NT - 1 else maskR
                    if m is not None:
                        p3 = V(p_sb, p_sb.handle[:, par * 3 + j, :]
                               .rearrange("p (a q) -> p a q", a=4),
                               (0, 128, (par * 3 + j) * 512, (par * 3 + j + 1) * 512))
                        o.tt("vector", p3, p3,
                             V(m, m.handle[:].unsqueeze(1).broadcast_to([128, 4, 128]),
                               (0, 128, 0, 128)), ALU.mult, tag=tag + "mk")
            for par in range(2):
                dbank = banks6[par]
                for j in range(3):
                    o.mm(V(dbank, dbank.handle[0:1, 0:512], (0, 1, 0, 512)),
                         V(onescol, onescol.handle[:], (0, 128, 0, 1)),
                         V(p_sb, p_sb.handle[:, par * 3 + j, :],
                           (0, 128, (par * 3 + j) * 512, (par * 3 + j + 1) * 512)),
                         start=(j == 0), stop=(j == 2), tag=tag + "dn")
                o.copy("vector", V(rec_sb, rec_sb.handle[0:1, par * 512:(par + 1) * 512],
                                   (0, 1, par * 512, (par + 1) * 512)),
                       V(dbank, dbank.handle[0:1, 0:512], (0, 1, 0, 512)), tag=tag + "dc")
            rv = V(rec_sb, rec_sb.handle[0:1, 0:1024], (0, 1, 0, 1024))
            o.act(rv, rv, AF.Ln, q="scalar", tag=tag + "rl")
            o.act(rv, rv, AF.Exp, scale=-1.0, q="scalar", tag=tag + "re")
            for par in range(2):
                dbank = banks6[par]
                tv = V(dtmp, dtmp.handle[0:1, par * 512:(par + 1) * 512],
                       (0, 1, par * 512, (par + 1) * 512))
                rvp = V(rec_sb, rec_sb.handle[0:1, par * 512:(par + 1) * 512],
                        (0, 1, par * 512, (par + 1) * 512))
                o.tt("vector", tv, V(dbank, dbank.handle[0:1, 0:512], (0, 1, 0, 512)),
                     rvp, ALU.mult, tag=tag + "r1")
                o.ts("vector", tv, tv, -1.0, ALU.mult, 2.0, ALU.add, tag=tag + "r2")
                o.tt("vector", rvp, rvp, tv, ALU.mult, tag=tag + "r3")
            for par in range(2):
                bb = banks6[2 + par]
                o.mm(V(bb, bb.handle[:, 0:512], (0, 128, 0, 512)),
                     V(onesrow, onesrow.handle[0:1, :], (0, 1, 0, 128)),
                     V(rec_sb, rec_sb.handle[0:1, par * 512:(par + 1) * 512],
                       (0, 1, par * 512, (par + 1) * 512)), tag=tag + "bc")
                for j in range(3):
                    pv = V(p_sb, p_sb.handle[:, par * 3 + j, :],
                           (0, 128, (par * 3 + j) * 512, (par * 3 + j + 1) * 512))
                    o.tt("vector", pv, pv, V(bb, bb.handle[:, 0:512], (0, 128, 0, 512)),
                         ALU.mult, tag=tag + "pn")
            for par in range(2):
                abank = banks6[4 + par]
                for hh in range(4):
                    h = hh * 2 + par
                    for j in range(3):
                        o.mm(V(abank, abank.handle[0:64, hh * 128:(hh + 1) * 128],
                               (0, 64, hh * 128, (hh + 1) * 128)),
                             V(v_tm, v_tm.handle[:, qbl + j, h * 64:(h + 1) * 64],
                               (0, 128, (qbl + j) * 512 + h * 64, (qbl + j) * 512 + (h + 1) * 64)),
                             V(p_sb, p_sb.handle[:, par * 3 + j, hh * 128:(hh + 1) * 128],
                               (0, 128, (par * 3 + j) * 512 + hh * 128,
                                (par * 3 + j) * 512 + (hh + 1) * 128)),
                             start=(j == 0), stop=(j == 2), tag=tag + "av")
            if dbg is not None and g == 1 and qbl == 0:
                for nm, b_ in [("p", p_sb), ("k", k_fm), ("v", v_tm), ("q", q_fm)]:
                    em.dma("sync", dbg[nm].ap(), b_.ap(), reads=[R(b_)], writes=[R(dbg[nm])])
            for par in range(2):
                abank = banks6[4 + par]
                o.copy("vector",
                       V(lt_fm, lt_fm.handle[par * 64:par * 64 + 64, 0:4, qbl * 128:(qbl + 1) * 128],
                         (par * 64, par * 64 + 64, 0, 4 * T)),
                       V(abank, abank.handle[0:64, 0:512].rearrange("p (a q) -> p a q", a=4),
                         (0, 64, 0, 512)), tag=tag + "oe")
        kflat = k_fm.handle[:].rearrange("p a b -> p (a b)")
        for lg in range(GRP // 512):
            def wo_evac(em_, psb, mt_g, mt_rel, lg_, l0, nl, _kflat=kflat):
                o.act(V(k_fm, _kflat[:, mt_g * 512 + l0:mt_g * 512 + l0 + nl],
                        (0, 128, mt_g * 512 + l0, mt_g * 512 + l0 + nl)),
                      V(psb, psb.handle[:, 0:nl], (0, 128, 0, nl)), AF.Copy,
                      q="scalar", tag=tag + "woe")
            emit_linear(em, W["wout"], lt_fm, None, 512, wb4, mk.ps,
                        evac_fn=wo_evac,
                        lg_size=512, tok0=0, src_tok0=lg * 512, src_ct0=0, tag=tag + "wo")
            for lt in range(4):
                tb = t_o[lt % 2]
                for ct in range(8):
                    psb = mk.ptrb[ct % 2]
                    o.transpose(V(psb, psb.handle[:, 0:128], (0, 128, 0, 128)),
                                V(k_fm, kflat[:, ct * 512 + lt * 128:ct * 512 + (lt + 1) * 128],
                                  (0, 128, ct * 512 + lt * 128, ct * 512 + (lt + 1) * 128)),
                                mk.ident, tag=tag + "wtT")
                    o.copy("vector", vw(tb, l0=ct * 128, nl=128),
                           V(psb, psb.handle[:, 0:128], (0, 128, 0, 128)), tag=tag + "wtE")
                gt = (t0g + lg * 512) // 128 + lt
                em.dma("sync", light_tm_scr.handle[gt], tb.ap(), reads=[R(tb)],
                       writes=[R(light_tm_scr, 0, 128, gt * D, (gt + 1) * D)],
                       tag=tag + "ws")





def stage_route(mk, s_q, s_kv, iq_dram, ikv_dram, tag="rt"):
    em, o = mk.em, mk.o
    from concourse.bass import IndirectOffsetOnAxis
    NQ, NKV = 1024, 2048
    KEFF = (1152.0, 2304.0)
    import math
    s2T = em.sbuf(tag + "_s2T", [64, 128], F32)
    wT = em.sbuf(tag + "_wT", [64, 128], F32)
    e2 = em.sbuf(tag + "_e2", [64, 128], F32)
    gt = em.sbuf(tag + "_gt", [64, 128], F32)
    tie = em.sbuf(tag + "_tie", [64, 128], F32)
    pref = em.sbuf(tag + "_pref", [64, 128], F32)
    ones2 = em.sbuf(tag + "_o2", [64, 128], F32)
    sel = em.sbuf(tag + "_sel", [64, 128], F32)
    M32 = em.sbuf(tag + "_m32", [2, 64], F32)
    M32T = em.sbuf(tag + "_m32t", [64, 2], F32)
    TRI = em.sbuf(tag + "_tri", [64, 64], F32)
    sc = em.sbuf(tag + "_sc", [2, 12], F32)
    cb = em.sbuf(tag + "_cb", [64, 8], F32)
    ps0, ps1 = mk.ps[0], mk.ps[1]

    def SC(j):
        return V(sc, sc.handle[0:2, j:j + 1], (0, 2, j, j + 1))

    def CB(j):
        return V(cb, cb.handle[0:64, j:j + 1], (0, 64, j, j + 1))

    def CBb(j):
        return V(cb, cb.handle[0:64, j:j + 1].broadcast_to([64, 128]), (0, 64, j, j + 1))

    full = lambda b: V(b, b.handle[0:64, 0:128], (0, 64, 0, 128))
    m32_np = np.zeros((2, 64), np.float32)
    m32_np[0, 0:32] = 1.0
    m32_np[1, 32:64] = 1.0
    m32_d = mk.dram_in(tag + "_m32c", m32_np)
    em.dma("sync", M32.ap(), m32_d.ap(), reads=[R(m32_d)], writes=[R(M32)],
           tag=tag + "c0")
    o.memset("vector", vw(M32T), 0.0, tag=tag + "c3")
    o.memset("vector", V(M32T, M32T.handle[0:32, 0:1], (0, 32, 0, 1)), 1.0, tag=tag + "c4")
    o.memset("vector", V(M32T, M32T.handle[32:64, 1:2], (32, 64, 1, 2)), 1.0, tag=tag + "c5")
    o.memset("vector", vw(ones2), 1.0, tag=tag + "c6")
    o.memset("vector", vw(TRI), 1.0, tag=tag + "c7")
    o.affine_select(vw(TRI), vw(TRI), [[1, 64]], ALU.is_ge, 0.0, base=-1,
                    channel_multiplier=-1, tag=tag + "c8")
    o.memset("vector", V(TRI, TRI.handle[0:32, 32:64], (0, 32, 32, 64)), 0.0, tag=tag + "c9")
    sc_np = np.zeros((2, 12), np.float32)
    for r in range(2):
        sc_np[r, 0] = math.log(KEFF[r]) - math.log(float(T))
        sc_np[r, 1] = math.log(KEFF[r])
        sc_np[r, 2] = float((NQ, NKV)[r])
    sc_np[:, 3] = -65536.0
    sc_np[:, 4] = 0.0
    sc_np[0, 10] = 1.0
    sc_np[1, 10] = 128.0
    sc_np[:, 11] = 1.0
    sc_d = mk.dram_in(tag + "_scc", sc_np)
    em.dma("sync", sc.ap(), sc_d.ap(), reads=[R(sc_d)], writes=[R(sc)],
           tag=tag + "a0")
    for r, s_sb in ((0, s_q), (1, s_kv)):
        psb = mk.ps[2 + r]
        o.transpose(V(psb, psb.handle[0:32, 0:128], (0, 32, 0, 128)), vw(s_sb), mk.identf,
                    tag=tag + "sT")
        o.copy("vector", V(s2T, s2T.handle[r * 32:(r + 1) * 32, 0:128],
                           (r * 32, (r + 1) * 32, 0, 128)),
               V(psb, psb.handle[0:32, 0:128], (0, 32, 0, 128)), tag=tag + "sE")

    P64 = lambda ps, c0, c1: V(ps, ps.handle[0:64, c0:c1], (0, 64, c0, c1))
    P2 = lambda ps, c0, c1: V(ps, ps.handle[0:2, c0:c1], (0, 2, c0, c1))

    def bcast_col(src_scV, dst_j, tg):
        o.mm(P64(ps0, 0, 1), V(M32, M32.handle[0:2, 0:64], (0, 2, 0, 64)), src_scV,
             tag=tg + "b")
        o.copy("vector", CB(dst_j), P64(ps0, 0, 1), tag=tg + "c")

    def count_routes(mask_buf, dst_scV, tg):
        o.reduce("vector", CB(7), full(mask_buf), ALU.add, tag=tg + "r")
        o.mm(P2(ps1, 0, 1), V(M32T, M32T.handle[0:64, 0:2], (0, 64, 0, 2)), CB(7),
             tag=tg + "m")
        o.copy("vector", dst_scV, P2(ps1, 0, 1), tag=tg + "c")

    for it in range(49):
        bcast_col(SC(0), 0, tag + f"a{it}")
        o.ts("vector", full(e2), full(s2T), CB(0), ALU.add, 0.0, ALU.min,
             tag=tag + "u")
        o.act(full(e2), full(e2), AF.Exp, tag=tag + "e")
        count_routes(e2, SC(5), tag + "z")
        o.act(SC(6), SC(5), AF.Ln, tag=tag + "ln")
        o.tt("vector", SC(0), SC(0), SC(6), ALU.subtract, tag=tag + "as")
        o.tt("vector", SC(0), SC(0), SC(1), ALU.add, tag=tag + "aa")
    bcast_col(SC(0), 0, tag + "af")
    o.ts("vector", full(wT), full(s2T), CB(0), ALU.add, 0.0, ALU.min,
         tag=tag + "wf")

    for it in range(46):
        o.tt("vector", SC(5), SC(3), SC(4), ALU.add, tag=tag + "m1")
        o.ts("vector", SC(5), SC(5), 0.5, ALU.mult, tag=tag + "m2")
        bcast_col(SC(5), 1, tag + f"m{it}")
        o.ts("vector", full(gt), full(wT), CB(1), ALU.is_gt, tag=tag + "g")
        count_routes(gt, SC(6), tag + "n")
        o.tt("vector", SC(7), SC(6), SC(2), ALU.is_ge, tag=tag + "p")
        o.tt("vector", SC(8), SC(5), SC(3), ALU.subtract, tag=tag + "d1")
        o.tt("vector", SC(8), SC(8), SC(7), ALU.mult, tag=tag + "d2")
        o.tt("vector", SC(3), SC(3), SC(8), ALU.add, tag=tag + "d3")
        o.ts("vector", SC(9), SC(7), -1.0, ALU.mult, 1.0, ALU.add, tag=tag + "q1")
        o.tt("vector", SC(8), SC(5), SC(4), ALU.subtract, tag=tag + "q2")
        o.tt("vector", SC(8), SC(8), SC(9), ALU.mult, tag=tag + "q3")
        o.tt("vector", SC(4), SC(4), SC(8), ALU.add, tag=tag + "q4")

    bcast_col(SC(4), 1, tag + "bh")
    bcast_col(SC(3), 2, tag + "bl")
    o.ts("vector", full(gt), full(wT), CB(1), ALU.is_gt, tag=tag + "sg")
    count_routes(gt, SC(6), tag + "ns")
    o.tt("vector", SC(7), SC(2), SC(6), ALU.subtract, tag=tag + "nv")
    bcast_col(SC(7), 3, tag + "bn")
    o.ts("vector", full(tie), full(wT), CB(2), ALU.is_gt, tag=tag + "t1")
    o.tt("vector", full(tie), full(tie), full(gt), ALU.subtract, tag=tag + "t2")

    def ex_prefix(mask_buf, out_buf, tg):
        o.scan_mult_add("vector", full(out_buf), full(ones2), full(mask_buf), 0.0,
                        tag=tg + "cs")
        o.tt("vector", full(out_buf), full(out_buf), full(mask_buf), ALU.subtract,
             tag=tg + "ex")
        o.reduce("vector", CB(7), full(mask_buf), ALU.add, tag=tg + "tt")
        o.mm(P64(ps0, 1, 2), V(TRI, TRI.handle[0:64, 0:64], (0, 64, 0, 64)), CB(7),
             tag=tg + "tp")
        o.copy("vector", CB(6), P64(ps0, 1, 2), tag=tg + "tc")
        o.tt("vector", full(out_buf), full(out_buf), CBb(6), ALU.add, tag=tg + "ad")

    ex_prefix(tie, pref, tag + "pt")
    o.tt("vector", full(e2), full(pref), CBb(3), ALU.is_lt, tag=tag + "s1")
    o.tt("vector", full(e2), full(e2), full(tie), ALU.mult, tag=tag + "s2")
    o.tt("vector", full(sel), full(gt), full(e2), ALU.add, tag=tag + "s3")
    ex_prefix(sel, pref, tag + "ps")
    o.ts("vector", full(e2), full(sel), -1e9, ALU.mult, 1e9, ALU.add, tag=tag + "x1")
    o.tt("vector", full(e2), full(e2), full(pref), ALU.add, tag=tag + "x2")
    idxfT = em.sbuf(tag + "_ifT", [128, 64], F32)
    for r in range(2):
        psb = mk.ps[2 + r]
        idsl = (V(mk.identf, mk.identf.handle[0:32, 0:32], (0, 32, 0, 32)) if r == 0
                else V(mk.identf, mk.identf.handle[32:64, 32:64], (32, 64, 32, 64)))
        o.transpose(V(psb, psb.handle[:, 0:32], (0, 128, 0, 512)),
                    V(e2, e2.handle[r * 32:(r + 1) * 32, 0:128],
                      (r * 32, (r + 1) * 32, 0, 128)), idsl, tag=tag + "xT")
        o.copy("vector", V(idxfT, idxfT.handle[:, r * 32:(r + 1) * 32],
                           (0, 128, r * 32, (r + 1) * 32)),
               V(psb, psb.handle[:, 0:32], (0, 128, 0, 512)), tag=tag + "xE")
    iotaT = em.sbuf(tag + "_iot", [128, 2048], F32)
    o.iota(V(iotaT, iotaT.handle[:], (0, 128, 0, 2048)), [[1, 2048]], base=0,
           channel_multiplier=0, allow_imprecise=True, tag=tag + "io")
    idc = em.sbuf(tag + "_idc", [128, 2], BF16)
    o.iota(V(idc, idc.handle[:, 0:1], (0, 128, 0, 1)), [[1, 1]], base=0,
           channel_multiplier=1, allow_imprecise=True, tag=tag + "ip")
    P1 = mk.wb[0]
    rows2 = em.sbuf(tag + "_r2", [2, 2048], F32)
    rowi = em.sbuf(tag + "_ri", [1, 2048], I32)
    for r, (dram, n) in enumerate(((iq_dram, NQ), (ikv_dram, NKV))):
        o.memset("vector", V(rows2, rows2.handle[0:2, 0:n], (0, 2, 0, n)), 0.0,
                 tag=tag + "cz")
        for tt in range(32):
            rcb = V(idxfT, idxfT.handle[:, r * 32 + tt:r * 32 + tt + 1]
                    .broadcast_to([128, n]), (0, 128, r * 32 + tt, r * 32 + tt + 1))
            o.tt("vector", V(P1, P1.handle[:, 0:n], (0, 128, 0, n)),
                 V(iotaT, iotaT.handle[:, 0:n], (0, 128, 0, n)), rcb, ALU.is_equal,
                 tag=tag + "p1")
            o.memset("vector", V(idc, idc.handle[:, 1:2], (0, 128, 1, 2)), float(tt),
                     tag=tag + "tc")
            for c in range(n // 512):
                psb = mk.ps[c]
                o.mm(V(psb, psb.handle[0:2, 0:512], (0, 2, 0, 512)),
                     V(idc, idc.handle[:, 0:2], (0, 128, 0, 2)),
                     V(P1, P1.handle[:, c * 512:(c + 1) * 512],
                       (0, 128, c * 512, (c + 1) * 512)),
                     start=True, stop=True, tag=tag + "cm")
                o.tt("vector", V(rows2, rows2.handle[0:2, c * 512:(c + 1) * 512],
                                 (0, 2, c * 512, (c + 1) * 512)),
                     V(rows2, rows2.handle[0:2, c * 512:(c + 1) * 512],
                       (0, 2, c * 512, (c + 1) * 512)),
                     V(psb, psb.handle[0:2, 0:512], (0, 2, 0, 512)), ALU.add,
                     tag=tag + "ca2")
        o.ts("vector", V(rows2, rows2.handle[0:2, 0:n], (0, 2, 0, n)),
             V(rows2, rows2.handle[0:2, 0:n], (0, 2, 0, n)), SC(10), ALU.mult,
             tag=tag + "cs")
        for c in range(n // 512):
            psb = mk.ps2[0]
            o.mm(V(psb, psb.handle[0:1, 0:512], (0, 1, 0, 512)), SC(11),
                 V(rows2, rows2.handle[0:2, c * 512:(c + 1) * 512],
                   (0, 2, c * 512, (c + 1) * 512)), tag=tag + "cr")
            o.copy("vector", V(rows2, rows2.handle[0:1, c * 512:(c + 1) * 512],
                               (0, 1, c * 512, (c + 1) * 512)),
                   V(psb, psb.handle[0:1, 0:512], (0, 1, 0, 512)), tag=tag + "cc")
        o.copy("vector", V(rowi, rowi.handle[0:1, 0:n], (0, 1, 0, n)),
               V(rows2, rows2.handle[0:1, 0:n], (0, 1, 0, n)), tag=tag + "ci")
        em.dma("sync", dram.handle[:].rearrange("a b -> b a"),
               rowi.handle[0:1, 0:n], reads=[R(rowi)], writes=[R(dram)],
               tag=tag + "cd")




def stage_heavy(mk, xn0_tm_scr, light_tm_scr, W, rmsg_b, nullq_b, nullkv_d,
                iq_dram, ikv_dram, rn_scr, g0_scr, tag="hv", upto="full", dbg=None):
    em, o = mk.em, mk.o
    from concourse.bass import IndirectOffsetOnAxis
    H, DH = 8, 64
    LKV = 2176
    st = mk.stats
    C = lambda j: V(st, st.handle[:, j:j + 1], (0, 128, j, j + 1))
    rnb = [em.sbuf(f"{tag}_rn{i}", [128, D], BF16) for i in range(2)]
    for t in range(NT):
        xb = mk.tbf[t % 2]
        em.dma("sync", xb.ap(), xn0_tm_scr.handle[t], reads=[R(xn0_tm_scr)],
               writes=[R(xb)], tag=tag + "lx")
        o.act(vw(mk.dump), vw(xb), AF.Square, accum=C(0), tag=tag + "sq")
        o.ts("vector", C(0), C(0), 1e-24, ALU.max, tag=tag + "cl")
        newton_rsqrt(mk, C(1), C(0), C(2))
        o.ts("vector", C(1), C(1), float(D) ** 0.5, ALU.mult, tag=tag + "sd")
        rb = rnb[t % 2]
        o.ts("vector", vw(mk.tf32[t % 2]), vw(xb), C(1), ALU.mult, tag=tag + "r1")
        o.tt("vector", vw(rb), vw(mk.tf32[t % 2]), vw(rmsg_b), ALU.mult, tag=tag + "r2")
        em.dma("sync", rn_scr.handle[t], rb.ap(), reads=[R(rb)],
               writes=[R(rn_scr, 0, 128, t * D, (t + 1) * D)], tag=tag + "sr")
    for t in range(NT):
        lb = mk.tbf[t % 2]
        em.dma("sync", lb.ap(), light_tm_scr.handle[t], reads=[R(light_tm_scr)],
               writes=[R(lb)], tag=tag + "ll")
        o.tt("vector", vw(mk.tf32[t % 2]), vw(lb), vw(nullq_b), ALU.add, tag=tag + "g0")
        em.dma("sync", g0_scr.handle[t], mk.tf32[t % 2].ap(), reads=[R(mk.tf32[t % 2])],
               writes=[R(g0_scr, 0, 128, t * D, (t + 1) * D)], tag=tag + "sg")
    if upto == "rn":
        return
    for (lin, wbuf, c0) in ((W["wq_h"], mk.wb[0], 0), (W["wkv_h"], mk.wb[0], 4096),
                            (W["wkv_h"], mk.wb[1], 0), (W["wout_h"], mk.wb[1], 4096),
                            (W["wout_h"], mk.wb[1], 6144)):
        pan = {0: 0, 4096: 0, 6144: 1}[c0] if lin is not W["wkv_h"] else (0 if c0 == 4096 else 1)
        ncols = lin.kt * lin.p_out
        em.dma("gpsimd", wbuf.handle[:, c0:c0 + ncols], lin.w[pan],
               reads=[R(lin.w)], writes=[R(wbuf, 0, 128, c0, c0 + ncols)], tag=tag + "wl")
    iq_sb = em.sbuf(tag + "_iq", [128, 8], I32)
    ikv_sb = em.sbuf(tag + "_ik", [128, 16], I32)
    for (sb, dram, n) in ((iq_sb, iq_dram, 8), (ikv_sb, ikv_dram, 16)):
        dr = dram.handle[:].rearrange("(j p) o -> j p o", p=128)
        for j in range(n):
            em.dma("sync", sb.handle[:, j:j + 1], dr[j], reads=[R(dram)],
                   writes=[R(sb, 0, 128, j, j + 1)], tag=tag + "li")
    xg_fm = [em.sbuf(f"{tag}_xg{i}", [128, 8, 128], BF16) for i in range(2)]
    qd = em.sbuf(tag + "_qd", [64, 8, 1024], BF16)
    kv_fm = em.sbuf(tag + "_kv", [128, 8, LKV], BF16)
    rnflat = rn_scr.handle[:].rearrange("t p d -> (t p) d")

    def gather_tile(sb, j, xf):
        xg = mk.tbf[j % 2]
        em.op("gpsimd",
              (lambda _o, _i, _x: lambda e: e.indirect_dma_start(
                  _o, None, _i, IndirectOffsetOnAxis(_x, 0)))(
                  xg.handle[:], rnflat, sb.handle[:, j:j + 1]),
              reads=[R(rn_scr), R(sb)], writes=[R(xg)], is_dma=True, tag=tag + "ga")
        emit_transpose_fm_from_tm(em, xg, xf, mk.ident, mk.ptrb, D, tok0=0,
                                  tag=tag + "gt")

    for j in range(8):
        xf = xg_fm[j % 2]
        gather_tile(iq_sb, j, xf)
        for m in range(4):
            psb = mk.ps[m % 4]
            for kk in range(8):
                o.mm(V(psb, psb.handle[:, 0:128], (0, 128, 0, 512)),
                     V(mk.wb[0], mk.wb[0].handle[:, kk * 512 + m * 128:kk * 512 + (m + 1) * 128],
                       (0, 128, kk * 512 + m * 128, kk * 512 + (m + 1) * 128)),
                     vw(xf, mt=kk, l0=0, nl=128),
                     start=(kk == 0), stop=(kk == 7), tag=tag + "pq")
            for hp in range(2):
                h = 2 * m + hp
                o.copy("vector",
                       V(qd, qd.handle[0:64, h, j * 128:(j + 1) * 128],
                         (0, 64, h * 1024 + j * 128, h * 1024 + (j + 1) * 128)),
                       V(psb, psb.handle[hp * 64:(hp + 1) * 64, 0:128],
                         (hp * 64, (hp + 1) * 64, 0, 512)), tag=tag + "pqe")
    for j in range(16):
        xf = xg_fm[j % 2]
        gather_tile(ikv_sb, j, xf)
        for pan in range(2):
            wbuf, c0 = (mk.wb[0], 4096) if pan == 0 else (mk.wb[1], 0)
            for m in range(4):
                psb = mk.ps[m % 4]
                for kk in range(8):
                    o.mm(V(psb, psb.handle[:, 0:128], (0, 128, 0, 512)),
                         V(wbuf, wbuf.handle[:, c0 + kk * 512 + m * 128:c0 + kk * 512 + (m + 1) * 128],
                           (0, 128, c0 + kk * 512 + m * 128, c0 + kk * 512 + (m + 1) * 128)),
                         vw(xf, mt=kk, l0=0, nl=128),
                         start=(kk == 0), stop=(kk == 7), tag=tag + "pk")
                mt = pan * 4 + m
                o.copy("scalar", vw(kv_fm, mt=mt, l0=1 + j * 128, nl=128),
                       V(psb, psb.handle[:, 0:128], (0, 128, 0, 512)), tag=tag + "pke")
    o.memset("vector", V(kv_fm, kv_fm.handle[:, :, 2049:LKV], (0, 128, 0, 8 * LKV)),
             0.0, tag=tag + "pz")
    nk4 = nullkv_d.handle[:].rearrange("a h (d o) -> a h d o", o=1)
    for h in range(H):
        for i in range(2):
            em.dma("gpsimd", kv_fm.handle[i * 64:(i + 1) * 64, h, 0:1], nk4[i, h],
                   reads=[R(nullkv_d)],
                   writes=[R(kv_fm, i * 64, (i + 1) * 64, h * LKV, h * LKV + 1)],
                   tag=tag + "nk")
    if upto == "gather":
        return
    va = em.sbuf(tag + "_va", [128, 17, 520], BF16)
    o.memset("vector", V(va, va.handle[:], (0, 128, 0, 17 * 520)), 1.0, tag=tag + "vo")
    for h in range(H):
        for kt in range(17):
            pt = mk.ptrb[kt % 2]
            o.transpose(V(pt, pt.handle[:, 0:64], (0, 128, 0, 64)),
                        V(kv_fm, kv_fm.handle[64:128, h, kt * 128:(kt + 1) * 128],
                          (64, 128, h * LKV + kt * 128, h * LKV + (kt + 1) * 128)),
                        V(mk.ident, mk.ident.handle[64:128, 64:128], (64, 128, 64, 128)),
                        tag=tag + "vT")
            o.copy("vector",
                   V(va, va.handle[:, kt, h * 65:h * 65 + 64],
                     (0, 128, kt * 520 + h * 65, kt * 520 + h * 65 + 64)),
                   V(pt, pt.handle[:, 0:64], (0, 128, 0, 64)), tag=tag + "vE")
    if upto == "proj":
        return
    p_sb = [em.sbuf(f"{tag}_p{i}", [128, LKV], BF16) for i in range(1)]
    pts = em.sbuf(tag + "_pt", [128, LKV], BF16)
    att_sb = [em.sbuf(f"{tag}_at{i}", [128, 64], BF16) for i in range(2)]
    o_fm = em.sbuf(tag + "_of", [128, 4, 1024], BF16)
    chunks = [(0, 512), (512, 1024), (1024, 1536), (1536, 2048), (2048, LKV)]
    for h in range(H):
        for qb in range(8):
            it = h * 8 + qb
            cb0 = 4 * (it % 2)
            pb = p_sb[0]
            psl = []
            for ci, (c0, c1) in enumerate(chunks):
                n = c1 - c0
                psb = mk.ps[ci] if ci < 4 else mk.ps2[0]
                psV = V(psb, psb.handle[:, 0:n], (0, 128, 0, 512))
                psl.append((psV, c0, n))
                o.mm(psV,
                     V(qd, qd.handle[0:64, h, qb * 128:(qb + 1) * 128],
                       (0, 64, h * 1024 + qb * 128, h * 1024 + (qb + 1) * 128)),
                     V(kv_fm, kv_fm.handle[0:64, h, c0:c1],
                       (0, 64, h * LKV + c0, h * LKV + c1)),
                     tag=tag + "sm")
            for ci, (psV, c0, n) in enumerate(psl):
                if ci == 0:
                    o.reduce("vector", C(cb0 + 0), psV, ALU.max, tag=tag + "x0")
                else:
                    o.reduce("vector", C(cb0 + 1), psV, ALU.max, tag=tag + "x1")
                    o.tt("vector", C(cb0 + 0), C(cb0 + 0), C(cb0 + 1), ALU.max,
                         tag=tag + "x2")
            o.ts("vector", C(cb0 + 1), C(cb0 + 0), -0.125, ALU.mult, tag=tag + "x3")
            for (psV, c0, n) in psl:
                o.act(V(pb, pb.handle[:, c0:c0 + n], (0, 128, c0, c0 + n)), psV,
                      AF.Exp, bias=C(cb0 + 1), scale=0.125, tag=tag + "ex")
            o.memset("vector", V(pb, pb.handle[:, 2049:LKV], (0, 128, 2049, LKV)),
                     0.0, tag=tag + "pz")
            for kt in range(17):
                pt = mk.ptrb[kt % 2]
                o.transpose(V(pt, pt.handle[:, 0:128], (0, 128, 0, 128)),
                            V(pb, pb.handle[:, kt * 128:(kt + 1) * 128],
                              (0, 128, kt * 128, (kt + 1) * 128)),
                            mk.ident, tag=tag + "pT")
                o.copy("vector",
                       V(pts, pts.handle[:, kt * 128:(kt + 1) * 128],
                         (0, 128, kt * 128, (kt + 1) * 128)),
                       V(pt, pt.handle[:, 0:128], (0, 128, 0, 128)), tag=tag + "pE")
            avV = V(mk.ps2[1], mk.ps2[1].handle[:, 0:65], (0, 128, 0, 512))
            for kt in range(17):
                o.mm(avV,
                     V(pts, pts.handle[:, kt * 128:(kt + 1) * 128],
                       (0, 128, kt * 128, (kt + 1) * 128)),
                     V(va, va.handle[:, kt, h * 65:(h + 1) * 65],
                       (0, 128, kt * 520 + h * 65, kt * 520 + (h + 1) * 65)),
                     start=(kt == 0), stop=(kt == 16), tag=tag + "av")
            denV = V(mk.ps2[1], mk.ps2[1].handle[:, 64:65], (0, 128, 0, 512))
            newton_recip(mk, C(cb0 + 2), denV, C(cb0 + 3))
            ab = att_sb[it % 2]
            o.ts("vector", vw(ab), V(mk.ps2[1], mk.ps2[1].handle[:, 0:64], (0, 128, 0, 512)),
                 C(cb0 + 2), ALU.mult, tag=tag + "dv")
            o.transpose(V(mk.ptrb[0], mk.ptrb[0].handle[0:64, 0:128], (0, 64, 0, 128)),
                        vw(ab), mk.ident, tag=tag + "aT")
            o.copy("vector",
                   V(o_fm, o_fm.handle[(h % 2) * 64:(h % 2) * 64 + 64, h // 2,
                                       qb * 128:(qb + 1) * 128],
                     ((h % 2) * 64, (h % 2) * 64 + 64, (h // 2) * 1024 + qb * 128,
                      (h // 2) * 1024 + (qb + 1) * 128)),
                   V(mk.ptrb[0], mk.ptrb[0].handle[0:64, 0:128], (0, 64, 0, 128)),
                   tag=tag + "aE")
    if upto == "attn":
        return
    vaflat = va.handle[:].rearrange("p a b -> p (a b)")
    for pan in range(2):
        for m in range(4):
            mt = pan * 4 + m
            for lg in range(2):
                psb = mk.ps[(mt * 2 + lg) % 4]
                for kt in range(4):
                    o.mm(V(psb, psb.handle[:, 0:512], (0, 128, 0, 512)),
                         V(mk.wb[1], mk.wb[1].handle[:, 4096 + pan * 2048 + kt * 512 + m * 128:
                                                     4096 + pan * 2048 + kt * 512 + (m + 1) * 128],
                           (0, 128, 4096 + pan * 2048 + kt * 512 + m * 128,
                            4096 + pan * 2048 + kt * 512 + (m + 1) * 128)),
                         vw(o_fm, mt=kt, l0=lg * 512, nl=512),
                         start=(kt == 0), stop=(kt == 3), tag=tag + "wo")
                o.copy("scalar",
                       V(va, vaflat[:, mt * 1024 + lg * 512:mt * 1024 + (lg + 1) * 512],
                         (0, 128, mt * 1024 + lg * 512, mt * 1024 + (lg + 1) * 512)),
                       V(psb, psb.handle[:, 0:512], (0, 128, 0, 512)), tag=tag + "woe")
    g0flat = g0_scr.handle[:].rearrange("t p d -> (t p) d")
    for j in range(8):
        tf = mk.tf32[j % 2]
        for ct in range(8):
            psb = mk.ptrb[ct % 2]
            o.transpose(V(psb, psb.handle[:, 0:128], (0, 128, 0, 128)),
                        V(va, vaflat[:, ct * 1024 + j * 128:ct * 1024 + (j + 1) * 128],
                          (0, 128, ct * 1024 + j * 128, ct * 1024 + (j + 1) * 128)),
                        mk.ident, tag=tag + "oT")
            o.copy("vector", vw(tf, l0=ct * 128, nl=128),
                   V(psb, psb.handle[:, 0:128], (0, 128, 0, 128)), tag=tag + "oE")
        o.tt("vector", vw(tf), vw(tf), vw(nullq_b), ALU.subtract, tag=tag + "dl")
        em.op("gpsimd",
              (lambda _o, _x, _i: lambda e: e.indirect_dma_start(
                  _o, IndirectOffsetOnAxis(_x, 0), _i, None, compute_op=ALU.add))(
                  g0flat, iq_sb.handle[:, j:j + 1], tf.handle[:]),
              reads=[R(tf), R(iq_sb), R(g0_scr)], writes=[R(g0_scr)], is_dma=True,
              tag=tag + "sc")
    if dbg is not None:
        for nm, b_ in [("kv", kv_fm), ("of", o_fm)]:
            if nm in dbg:
                em.dma("sync", dbg[nm].ap(), b_.ap(), reads=[R(b_)], writes=[R(dbg[nm])])


import sys as _sys
if "/opt/trn_rl_repo" not in _sys.path:
    _sys.path.insert(0, "/opt/trn_rl_repo")
import ml_dtypes
from concourse.bass_utils import run_bass_kernel_spmd

NCORES = 8
IDENT_BF = np.eye(128).astype(ml_dtypes.bfloat16)
IDENT_F32 = np.eye(128, dtype=np.float32)


def _row(v):
    return np.ascontiguousarray(np.asarray(v, np.float32).reshape(1, D))


def _bcast_row(mk, name, v):
    em = mk.em
    dd = em.dram_in(name + "_h", [1, D], F32)
    sb = em.sbuf(name, [128, D], F32)
    em.dma("sync", sb.ap(), dd.handle[:].partition_broadcast(128), reads=[R(dd)],
           writes=[R(sb)])
    mk.in_map[name + "_h"] = _row(v)
    return sb


def _ident(mk):
    em = mk.em
    ident_in = em.dram_in("identin", [128, 128], BF16)
    em.dma("sync", mk.ident.ap(), ident_in.ap(), reads=[R(ident_in)],
           writes=[R(mk.ident)])
    mk.in_map["identin"] = IDENT_BF


def _identf(mk):
    em = mk.em
    d = em.dram_in("identfin", [128, 128], F32)
    em.dma("sync", mk.identf.ap(), d.ap(), reads=[R(d)], writes=[R(mk.identf)])
    mk.in_map["identfin"] = IDENT_F32


def kernel(**inp):
    f = lambda k: np.ascontiguousarray(np.asarray(inp[k], np.float32))
    x_all = f("L0_attn_x")
    B = x_all.shape[0]

    nc = bass.Bass("TRN2", target_bir_lowering=False, debug=False)
    mk = MK(nc, wb_cols=0)
    em, o = mk.em, mk.o
    x_in = em.dram_in("x", [T, D], F32)
    out_d = em.dram_out("out", [T, D], F32)
    xn0_tm = em.dram_tmp("xn0_tm", [NT, 128, D], BF16)
    xn2_tm = em.dram_tmp("xn2_tm", [NT, 128, D], BF16)
    s0_tm = em.dram_tmp("s0_tm", [NT, 128, D], BF16)
    light_tm = em.dram_tmp("light_tm", [NT, 128, D], BF16)
    g0_tm = em.dram_tmp("g0_tm", [NT, 128, D], F32)
    x1_tm = em.dram_tmp("x1_tm", [NT, 128, D], F32)
    xn1_tm = em.dram_tmp("xn1_tm", [NT, 128, D], BF16)
    s1_tm = em.dram_tmp("s1_tm", [NT, 128, D], BF16)
    g1_tm = em.dram_tmp("g1_tm", [NT, 128, D], BF16)
    rn_scr = em.dram_tmp("rn", [NT, 128, D], BF16)
    add1_scr = em.dram_tmp("add1_scr", [NT, 128, D], BF16)
    attn_scr = em.dram_tmp("attn_scr", [NT, 128, D], BF16)
    iq_d = em.dram_tmp("iq", [1024, 1], I32)
    ikv_d = em.dram_tmp("ikv", [2048, 1], I32)
    _ident(mk)
    _identf(mk)
    s_q = em.sbuf("s_q", [128, NT], F32)
    s_kv = em.sbuf("s_kv", [128, NT], F32)

    em.push_scope()
    rtq_b = _bcast_row(mk, "rtq_b", f("L0_ffn_rt_q")[0])
    rtkv_b = _bcast_row(mk, "rtkv_b", f("L0_ffn_rt_kv")[0])
    Rcols = em.sbuf("Rcols", [128, 2], F32)
    o.reduce("vector", V(Rcols, Rcols.handle[:, 0:1], (0, 128, 0, 1)), vw(rtq_b),
             ALU.add)
    o.reduce("vector", V(Rcols, Rcols.handle[:, 1:2], (0, 128, 1, 2)), vw(rtkv_b),
             ALU.add)
    lng_b = _bcast_row(mk, "lng_b", f("L0_attn_ln_g"))
    lnb_b = _bcast_row(mk, "lnb_b", f("L0_attn_ln_b"))
    x2buf = [em.sbuf(f"ln0_x2{i}", [128, D], BF16) for i in range(2)]
    stage_ln0(mk, x_in, xn0_tm, None, s_q, s_kv, rtq_b, rtkv_b,
              V(Rcols, Rcols.handle[:, 0:1], (0, 128, 0, 1)),
              V(Rcols, Rcols.handle[:, 1:2], (0, 128, 1, 2)),
              ln2={"gamma_b": lng_b, "beta_b": lnb_b, "eps": 1e-5,
                   "tm_scr": xn2_tm, "buf": x2buf})
    em.pop_scope()

    em.push_scope()
    mk.wb = [em.sbuf(f"p3_wb{i}", [128, 4096], BF16) for i in range(2)]
    Wrw = {nm: Linear(em, nm, f("L0_attn_" + nm), p_out=512)
           for nm in ("Wx", "Ww", "Wk", "Wv", "Wr", "Wo")}
    bw_in = em.dram_in("bw", [128, 8], F32)
    bw_sb = em.sbuf("bw_sb", [128, 8], F32)
    em.dma("sync", bw_sb.ap(), bw_in.ap(), reads=[R(bw_in)], writes=[R(bw_sb)])
    mk.in_map["bw"] = vec_mt(f("L0_attn_bw"))
    xn2_fm = em.sbuf("xn2_fm", [128, 8, T], BF16)
    for t in range(NT):
        tb = mk.tbf[t % 2]
        em.dma("sync", tb.ap(), xn2_tm.handle[t], reads=[R(xn2_tm)], writes=[R(tb)])
        emit_transpose_fm_from_tm(em, tb, xn2_fm, mk.ident, mk.ptrb, D, tok0=t * 128)
    stage_rwkv(mk, xn2_fm, Wrw, s0_tm, bw_sb)
    for lin in Wrw.values():
        mk.in_map.update(lin.in_map())
    em.pop_scope()

    em.push_scope()
    mk.wb = [em.sbuf(f"p4_wb{i}", [128, 4096], BF16) for i in range(2)]
    vecs = {"lgamma_b": _bcast_row(mk, "lgamma_b", f("L0_ffn_ln_g")),
            "lbeta_b": _bcast_row(mk, "lbeta_b", f("L0_ffn_ln_b"))}
    Wl = {"qkv": Linear(em, "qkv", f("L0_ffn_w_qkv_l"), p_out=512),
          "wout": Linear(em, "wout", f("L0_ffn_w_out_l"), p_out=512)}
    lnL_fm = em.sbuf("lnL_fm", [128, 8, T], BF16)
    stage_light(mk, xn0_tm, lnL_fm, Wl, vecs, light_tm)
    for lin in Wl.values():
        mk.in_map.update(lin.in_map())
    em.pop_scope()

    em.push_scope()
    mk.wb = [em.sbuf(f"p5_wb{i}", [128, 8192], BF16) for i in range(2)]
    rmsg_b = _bcast_row(mk, "rmsg_b", f("L0_ffn_rms_g"))
    nullq_b = _bcast_row(mk, "nullq_b", np.asarray(f("L0_ffn_null_q")).reshape(D))
    nullkv_d = mk.dram_in("nullkv", f("L0_ffn_null_kv"))
    Wh = {"wq_h": Linear(em, "wqh", f("L0_ffn_w_q_h"), p_out=512),
          "wkv_h": Linear(em, "wkvh", f("L0_ffn_w_kv_h").reshape(D, 1024), p_out=512),
          "wout_h": Linear(em, "wouth", f("L0_ffn_w_out_h"), p_out=512)}
    stage_route(mk, s_q, s_kv, iq_d, ikv_d)
    stage_heavy(mk, xn0_tm, light_tm, Wh, rmsg_b, nullq_b, nullkv_d, iq_d, ikv_d,
                rn_scr, g0_tm)
    for lin in Wh.values():
        mk.in_map.update(lin.in_map())
    em.pop_scope()

    em.push_scope()
    ln1_fm = em.sbuf("ln1_fm", [128, 8, T], BF16)
    em.push_scope()
    gamma1_b = _bcast_row(mk, "gamma1_b", f("L1_attn_gamma1"))
    beta1_b = _bcast_row(mk, "beta1_b", f("L1_attn_beta1"))
    l1buf = [em.sbuf(f"c0_l1{i}", [128, D], BF16) for i in range(2)]
    stage_combine(mk, x_in, True, s0_tm, g0_tm, x1_tm, out=None,
                  ln_outs={"xn_tm_scr": xn1_tm,
                           "ln1": {"gamma_b": gamma1_b, "beta_b": beta1_b,
                                   "eps": 1e-6, "buf": l1buf, "fm_sb": ln1_fm,
                                   "tm_scr": None}},
                  eps=1e-5, g_is_f32=True, tag="c0")
    em.pop_scope()

    em.push_scope()
    mk.wb = [em.sbuf(f"p7_wb{i}", [128, 8192], BF16) for i in range(2)]
    vecs = {}
    for nm, v, n in [("b1_mt", f("L1_attn_b1"), 32), ("b2_mt", f("L1_attn_b2"), 8)]:
        dd = em.dram_in(nm + "_h", [128, n], F32)
        sb = em.sbuf(nm, [128, n], F32)
        em.dma("sync", sb.ap(), dd.ap(), reads=[R(dd)], writes=[R(sb)])
        vecs[nm] = sb
        mk.in_map[nm + "_h"] = vec_mt(v)
    vecs["gamma1_b"] = _bcast_row(mk, "la_gamma1_b", f("L1_attn_gamma1"))
    vecs["beta1_b"] = _bcast_row(mk, "la_beta1_b", f("L1_attn_beta1"))
    Wl1 = {"wq": Linear(em, "wq", f("L1_attn_wq").reshape(D, D), p_out=512),
           "wk": Linear(em, "wk", f("L1_attn_wk").reshape(D, D), p_out=512),
           "wv": Linear(em, "wv", f("L1_attn_wv").reshape(D, D), p_out=512),
           "wo": Linear(em, "wo", f("L1_attn_wo").reshape(D, D), p_out=512),
           "w1": Linear(em, "w1", f("L1_attn_w1"), p_out=256),
           "w2": Linear(em, "w2", f("L1_attn_w2"), p_out=256)}
    stage_l1attn(mk, ln1_fm, xn1_tm, Wl1, vecs, add1_scr, attn_scr, s1_tm)
    for lin in Wl1.values():
        mk.in_map.update(lin.in_map())
    em.pop_scope()
    em.pop_scope()

    em.push_scope()
    mk.wb = [em.sbuf(f"p8_wb{i}", [128, 8192], BF16) for i in range(2)]
    TG = 512
    lin1 = Linear(em, "f1", f("L1_ffn_w1_w").T, p_out=512)
    lin2 = Linear(em, "f2", f("L1_ffn_w2_w").T, p_out=512)
    linh1 = Linear(em, "fh1", f("L1_ffn_h1_w").T, p_out=256)
    linh2 = Linear(em, "fh2", f("L1_ffn_h2_w").T, p_out=256)
    lin3 = Linear(em, "f3", f("L1_ffn_w3_w").T, p_out=256)
    ssb = {}
    for nm, key in [("s1", "L1_ffn_w1_s"), ("s2", "L1_ffn_w2_s"),
                    ("sh1", "L1_ffn_h1_s"), ("sh2", "L1_ffn_h2_s"),
                    ("s3", "L1_ffn_w3_s")]:
        v = f(key)
        dd = em.dram_in("tf_" + nm, [128, len(v) // 128], F32)
        ssb[nm] = em.sbuf("tf_" + nm + "_sb", [128, len(v) // 128], F32)
        em.dma("sync", ssb[nm].ap(), dd.ap(), reads=[R(dd)], writes=[R(ssb[nm])])
        mk.in_map["tf_" + nm] = vec_mt(v)
    wbig = [em.sbuf(f"tf_wbig{i}", [128, 4096], BF16) for i in range(2)]
    x_fm = em.sbuf("tf_xfm", [128, 8, TG], BF16)
    hA = em.sbuf("tf_hA", [128, 32, TG], BF16)
    hB = em.sbuf("tf_hB", [128, 32, TG], BF16)
    apan = em.sbuf("tf_ap", [128, 4, TG], BF16)
    bpan = em.sbuf("tf_bp", [128, 4, TG], BF16)
    outg = em.sbuf("tf_og", [128, 8, TG], BF16)
    for tg in range(T // TG):
        t0g = tg * TG
        for lt in range(TG // 128):
            tm = mk.tbf[lt % 2]
            em.dma("sync", tm.ap(), xn1_tm.handle[tg * 4 + lt], reads=[R(xn1_tm)],
                   writes=[R(tm)], tag="tfx")
            emit_transpose_fm_from_tm(em, tm, x_fm, mk.ident, mk.ptrb, D,
                                      tok0=lt * 128, tag="tfT")
        for pan in range(lin1.n_pan):
            emit_linear(em, lin1, x_fm, apan, TG, [wbig[0]], mk.ps, act=AF.Silu,
                        scale=ssb["s1"], lg_size=512, tok0=0, src_tok0=0,
                        pan_range=(pan, pan + 1), dst_mt0=0, tag="tf1")
            emit_linear(em, lin2, x_fm, bpan, TG, [wbig[1]], mk.ps, act=AF.Copy,
                        scale=ssb["s2"], lg_size=512, tok0=0, src_tok0=0,
                        pan_range=(pan, pan + 1), dst_mt0=0, tag="tf2",
                        psum_start_idx=2)
            for m in range(4):
                mt = pan * 4 + m
                o.tt("vector", vw(hA, mt=mt, l0=0, nl=512),
                     vw(apan, mt=m, l0=0, nl=512),
                     vw(bpan, mt=m, l0=0, nl=512), ALU.mult, tag="tfh")
        emit_linear(em, linh1, hA, hB, TG, mk.wb, mk.ps, act=AF.Gelu,
                    scale=ssb["sh1"], lg_size=512, tag="tfh1")
        emit_linear(em, linh2, hB, hA, TG, mk.wb, mk.ps, act=AF.Gelu,
                    scale=ssb["sh2"], lg_size=512, tag="tfh2")
        emit_linear(em, lin3, hA, outg, TG, mk.wb, mk.ps, act=AF.Copy,
                    scale=ssb["s3"], lg_size=512, tag="tf3")
        for lt in range(TG // 128):
            tm = mk.tbf[lt % 2]
            for ct in range(8):
                psb = mk.ptrb[ct % 2]
                o.transpose(V(psb, psb.handle[:, 0:128], (0, 128, 0, 128)),
                            vw(outg, mt=ct, l0=lt * 128, nl=128), mk.ident, tag="tfoT")
                o.copy("vector", vw(tm, l0=ct * 128, nl=128),
                       V(psb, psb.handle[:, 0:128], (0, 128, 0, 128)), tag="tfoE")
            em.dma("sync", g1_tm.handle[tg * 4 + lt], tm.ap(), reads=[R(tm)],
                   writes=[R(g1_tm, 0, 128, (tg * 4 + lt) * D, (tg * 4 + lt + 1) * D)],
                   tag="tfoS")
    for lin in (lin1, lin2, linh1, linh2, lin3):
        mk.in_map.update(lin.in_map())
    em.pop_scope()

    em.push_scope()
    stage_combine(mk, x1_tm, False, s1_tm, g1_tm, None, out=out_d, ln_outs=None,
                  eps=1e-5, g_is_f32=False, tag="c1")
    em.pop_scope()

    em.emit()
    maps = []
    for i in range(B):
        m = dict(mk.in_map)
        m["x"] = np.ascontiguousarray(x_all[i])
        maps.append(m)
    res = run_bass_kernel_spmd(nc, maps, core_ids=list(range(NCORES)))
    out = np.stack([np.asarray(res.results[i]["out"], np.float32) for i in range(B)],
                   axis=0)
    return out
